# version 66
# baseline (speedup 1.0000x reference)
import contextlib
import numpy as np
import ml_dtypes
import concourse.bass as bass
import concourse.mybir as mybir
from concourse.bass_utils import run_bass_kernel_spmd

F32 = mybir.dt.float32
BF16 = mybir.dt.bfloat16
I32 = mybir.dt.int32
I8 = mybir.dt.int8
AF = mybir.ActivationFunctionType
ALU = mybir.AluOpType

B, L, E, H, D = 2, 8192, 1024, 16, 64
HID = 3072
NCORES = 8
WIN = 8192
QOWN = 2048
Q0 = WIN - QOWN
LN_EPS = 1e-5
DEQ = 1.0 / 127.0
NEG = -1.0e30
BAND = 130.0
NPASS, HPP = 4, 4
SLOPES = [2.0 ** (-(h + 1) / 2.0) for h in range(H)]

ENGINES = ("sync", "gpsimd", "tensor", "scalar", "vector")


class Sem:
    def __init__(self, name):
        self.name = name
        self.handle = None
        self.n = 0


class Prog:
    def __init__(self):
        self.items = {e: [] for e in ENGINES}

    def op(self, eng, fn, sem=None, inc=1):
        if sem is None:
            self.items[eng].append(("op", fn, None, 0))
        else:
            self.items[eng].append(("op", fn, sem, inc))
            sem.n += inc

    def wait(self, eng, sem, val=None):
        v = sem.n if val is None else val
        if v > 0:
            self.items[eng].append(("wait", None, sem, v))

    def verify(self):
        ptr = {e: 0 for e in ENGINES}
        vals = {}
        progress = True
        while progress:
            progress = False
            for e in ENGINES:
                while ptr[e] < len(self.items[e]):
                    kind, fn, sem, v = self.items[e][ptr[e]]
                    if kind == "wait" and vals.get(id(sem), 0) < v:
                        break
                    if kind == "op" and sem is not None:
                        vals[id(sem)] = vals.get(id(sem), 0) + v
                    ptr[e] += 1
                    progress = True
        stuck = {e: ptr[e] for e in ENGINES if ptr[e] < len(self.items[e])}
        if stuck:
            lines = []
            for e, pt in stuck.items():
                kind, fn, sem, v = self.items[e][pt]
                lines.append(f"  {e}@{pt}/{len(self.items[e])}: wait "
                             f"{sem.name} >= {v} (have {vals.get(id(sem), 0)})")
            raise RuntimeError("DEADLOCK:\n" + "\n".join(lines))

    def emit(self, block):
        def mk(eng_name):
            def inner(eng):
                for kind, fn, sem, v in self.items[eng_name]:
                    if kind == "wait":
                        eng.wait_ge(sem.handle, v)
                    else:
                        inst = fn(eng)
                        if sem is not None:
                            inst.then_inc(sem.handle, v)
            return inner
        for e in ENGINES:
            getattr(block, e)(mk(e))


class Slots:
    def __init__(self, n):
        self.rec = [None] * n

    def wait_free(self, p, eng, i):
        r = self.rec[i]
        if r is not None:
            for sem, v in r:
                p.wait(eng, sem, v)

    def set_free(self, i, *sems):
        self.rec[i] = [(sem, sem.n) for sem in sems]


def band_lo_kt(h_global, q0):
    band = BAND / SLOPES[h_global]
    return max(0, int(np.floor((q0 - band) / 128.0)))


def build_nc():
    nc = bass.Bass("TRN2", target_bir_lowering=False, debug=False)

    xT_d = nc.dram_tensor("xT", [E, WIN], F32, kind="ExternalInput").ap()
    xo_d = nc.dram_tensor("xo", [QOWN, E], F32, kind="ExternalInput").ap()
    ka_d = nc.dram_tensor("kac", [NPASS, 8, 3, HPP * 1024], BF16, kind="ExternalInput").ap()
    qaug_d = nc.dram_tensor("qaug", [H, 4, QOWN], BF16, kind="ExternalInput").ap()
    masks_d = nc.dram_tensor("masks", [4, 128, 512], F32, kind="ExternalInput").ap()
    id_d = nc.dram_tensor("identc", [128, 128], BF16, kind="ExternalInput").ap()
    wkv_d = nc.dram_tensor("wkvp", [NPASS, E, 512], F32, kind="ExternalInput").ap()
    wq_d = nc.dram_tensor("wqp", [NPASS, E, 256], F32, kind="ExternalInput").ap()
    wo_d = nc.dram_tensor("wop", [E, E], F32, kind="ExternalInput").ap()
    wg_d = nc.dram_tensor("wgp", [24, 128, 1024], I8, kind="ExternalInput").ap()
    wu_d = nc.dram_tensor("wup", [24, 128, 1024], I8, kind="ExternalInput").ap()
    wd_d = nc.dram_tensor("wdp", [12, 128, 2048], I8, kind="ExternalInput").ap()
    y_d = nc.dram_tensor("y", [QOWN, E], F32, kind="ExternalOutput").ap()

    xn_img = nc.dram_tensor("xn_img", [E, WIN], BF16).ap()
    ctxT_img = nc.dram_tensor("ctxT_img", [E, QOWN], BF16).ap()
    s_img = nc.dram_tensor("s_img", [QOWN, E], F32).ap()

    with contextlib.ExitStack() as stack:
        ent = stack.enter_context

        ab = ent(nc.sbuf_tensor("ab", [128, 57216], BF16))
        af = ent(nc.sbuf_tensor("af", [128, 20608], F32))

        class Arena:
            def __init__(self, t):
                self.t, self.off, self.marks = t, 0, []

            def alloc(self, cols, part=128):
                a = self.off
                self.off += cols
                assert self.off <= self.t.shape[1], (self.t.shape[1], self.off)
                return self.t[0:part, a:a + cols]

            def mark(self):
                self.marks.append(self.off)

            def release(self):
                self.off = self.marks.pop()

        AB, AFA = Arena(ab), Arena(af)

        ident = AB.alloc(128)
        ones_col = AB.alloc(1)
        ones_row = AB.alloc(128, part=1)
        masks = [AFA.alloc(512) for _ in range(4)]
        ctxa = AFA.alloc(4160)
        rec = AFA.alloc(16)
        ctxn = [AB.alloc(256) for _ in range(2)]
        ctt = [AB.alloc(128) for _ in range(8)]

        AB.mark(); AFA.mark()
        xt = [AFA.alloc(512) for _ in range(8)]
        xtq = AFA.alloc(2048)
        ones_rf = AFA.alloc(128, part=1)
        mbc_f = AFA.alloc(512)
        rbc_f = AFA.alloc(512)
        t1f = AFA.alloc(512)
        t1g = AFA.alloc(512)
        strow = AFA.alloc(2048, part=1)
        srow2 = AFA.alloc(512, part=1)
        xb16 = [AB.alloc(4096) for _ in range(2)]
        sq16 = AB.alloc(4096)
        xn16 = [AB.alloc(4096) for _ in range(2)]
        wkv_sb = AB.alloc(4096)
        wq_sb = AB.alloc(2048)
        K_aug = [AB.alloc(4096, part=96) for _ in range(2)]
        V_aug = [AB.alloc(2080) for _ in range(2)]
        Q_aug = AB.alloc(8192, part=96)
        kqst = [AB.alloc(512) for _ in range(4)]
        PT = [AB.alloc(1024) for _ in range(6)]

        def ctx_acc(hl, su):
            base = hl * 16 + su
            return ctxa[:, base * 65:base * 65 + 65]

        ps_s = [ent(nc.psum_tensor(f"ps_s{i}", [128, 1024], F32)) for i in range(2)]
        ps_c = [ent(nc.psum_tensor(f"ps_c{i}", [128, 512], F32)) for i in range(2)]
        ps_p = ent(nc.psum_tensor("ps_p", [128, 512], F32))
        ps_l = ent(nc.psum_tensor("ps_l", [128, 512], F32))
        ps_lt = ps_l[0:2, :]
        ps_tp = ps_l.bitcast(BF16)[:, 0:128]

        sem_names = [
            "ini", "xld0", "xld1", "xld2", "xld3", "xld4", "xld5", "xld6", "xld7", "cast", "sq", "stat", "row", "rstd", "bcm", "bce",
            "app", "xnl", "xns", "prj", "pev", "krs0", "krs1", "qrs6", "qrs7", "kb0", "kb1", "wld", "qau",
            "sc", "msk", "exp", "ctx", "cev", "recs", "nrm", "tp", "tpe", "cts",
            "qld", "ctl", "ctw0", "ctw1", "wol", "smm", "sev", "sst", "sld0", "sld1", "fxl", "wgl0", "wgl1", "wgl2", "wul0", "wul1", "wul2", "wdl0", "wdl1", "wdl2", "dqw", "gmm", "sgl",
            "hml", "dmm", "gsl", "xol0", "xol1", "xol2", "xol3", "sld2", "sld3", "cmb", "yst",
        ]
        S = {n: Sem(n) for n in sem_names}
        p = Prog()

        p.op("gpsimd", lambda e: e.dma_start(out=ident[:], in_=id_d),
             S["wld"], 16)
        for j in range(4):
            p.op("gpsimd", lambda e, j=j: e.dma_start(
                out=masks[j][:], in_=masks_d[j]), S["wld"], 16)
        for bi in range(2):
            p.op("vector", lambda e, bi=bi: e.memset(K_aug[bi][64:96, :], 0.0))
            p.op("vector", lambda e, bi=bi: e.memset(K_aug[bi][64:65, :], 1.0))
            p.op("vector", lambda e, bi=bi: e.memset(
                V_aug[bi][:].rearrange("p (k h c) -> p k h c", h=HPP, c=65)[:, :, :, 64:65],
                1.0), S["ini"])
        p.op("vector", lambda e: e.memset(Q_aug[64:96, :], 0.0), S["ini"])
        p.op("vector", lambda e: e.memset(ones_col[:], 1.0), S["ini"])
        p.op("vector", lambda e: e.memset(ones_row[:], 1.0), S["ini"])
        p.op("vector", lambda e: e.memset(ones_rf[:], 1.0), S["ini"])

        sl_KV0 = Slots(2)
        sl_KV0.set_free(0, S["ini"])
        sl_KV0.set_free(1, S["ini"])
        sl_xt = Slots(8)
        sl_xb = Slots(2)
        sl_xn = Slots(2)
        sl_kq = Slots(4)
        sl_KV = Slots(2)
        sl_PT = Slots(6)
        sl_ss = Slots(2)
        sl_cs = Slots(2)
        sl_pj = Slots(2)
        sl_ctxn = Slots(2)
        sl_ctt = Slots(8)

        tile_i = [0]
        xn_done = set()

        def pass_lo_chunk(pp):
            return max(0, band_lo_kt(pp * HPP + HPP - 1, Q0) // 8)

        def emit_ln_fillers(t, c, half, hidx, xns):
            xbs = xb16[hidx % 2]
            sls = [(tile_i[0] + eg) % 8 for eg in range(8)]
            tile_i[0] += 8
            app_cell = [None]

            def L0():
                for eg in range(8):
                    sl = sls[eg]
                    sl_xt.wait_free(p, "sync", sl)
                    p.op("sync", lambda e, eg=eg, c=c, half=half, sl=sl:
                         e.dma_start(
                             out=xt[sl][:],
                             in_=xT_d[eg * 128:(eg + 1) * 128,
                                      c * 1024 + half * 512:
                                      c * 1024 + half * 512 + 512]),
                         S[f"xld{sl}"], 16)

            def L1():
                sl_xb.wait_free(p, "vector", hidx % 2)
                for eg in range(4):
                    sl = sls[eg]
                    p.wait("vector", S[f"xld{sl}"])
                    p.op("vector", lambda e, eg=eg, xbs=xbs, sl=sl:
                         e.tensor_copy(xbs[:, eg * 512:(eg + 1) * 512],
                                       xt[sl][:]), S["cast"])

            def L2():
                for eg in range(4, 8):
                    sl = sls[eg]
                    p.wait("vector", S[f"xld{sl}"])
                    p.op("vector", lambda e, eg=eg, xbs=xbs, sl=sl:
                         e.tensor_copy(xbs[:, eg * 512:(eg + 1) * 512],
                                       xt[sl][:]), S["cast"])
                p.op("vector", lambda e, xbs=xbs: e.tensor_tensor(
                    out=sq16[:], in0=xbs[:], in1=xbs[:], op=ALU.mult), S["sq"])

            def L3():
                sl_pj.wait_free(p, "tensor", 0)
                sl_pj.wait_free(p, "tensor", 1)
                p.wait("tensor", S["cast"])
                p.wait("tensor", S["ini"])
                for eg in range(8):
                    p.op("tensor", lambda e, eg=eg, xbs=xbs: e.matmul(
                        ps_p[0:1, :], ones_col[:],
                        xbs[:, eg * 512:(eg + 1) * 512],
                        start=(eg == 0), stop=(eg == 7)),
                        S["stat"] if eg == 7 else None)
                p.wait("tensor", S["sq"])
                for eg in range(8):
                    p.op("tensor", lambda e, eg=eg: e.matmul(
                        ps_l[0:1, :], ones_col[:],
                        sq16[:, eg * 512:(eg + 1) * 512],
                        start=(eg == 0), stop=(eg == 7)),
                        S["stat"] if eg == 7 else None)
                p.wait("vector", S["stat"])
                p.op("vector", lambda e: e.tensor_scalar_mul(
                    out=strow[0:1, 0:512], in0=ps_p[0:1, :], scalar1=1.0 / E))
                p.op("vector", lambda e: e.tensor_scalar_mul(
                    out=strow[0:1, 512:1024], in0=ps_l[0:1, :],
                    scalar1=1.0 / E), S["row"])
                p.op("vector", lambda e: e.tensor_tensor(
                    out=strow[0:1, 1024:1536], in0=strow[0:1, 0:512],
                    in1=strow[0:1, 0:512], op=ALU.mult))
                p.op("vector", lambda e: e.tensor_tensor(
                    out=strow[0:1, 1536:2048], in0=strow[0:1, 512:1024],
                    in1=strow[0:1, 1024:1536], op=ALU.subtract))
                p.op("vector", lambda e: e.tensor_scalar_add(
                    out=strow[0:1, 1536:2048], in0=strow[0:1, 1536:2048],
                    scalar1=LN_EPS), S["row"])

            def L4():
                p.wait("scalar", S["row"])
                p.op("scalar", lambda e: e.activation(
                    out=srow2[0:1, :], in_=strow[0:1, 1536:2048], func=AF.Ln))
                p.op("scalar", lambda e: e.activation(
                    out=srow2[0:1, :], in_=srow2[0:1, :], func=AF.Exp,
                    scale=-0.5), S["rstd"])
                p.wait("tensor", S["row"])
                p.op("tensor", lambda e: e.matmul(
                    ps_p[:], ones_rf[:], strow[0:1, 0:512], start=True,
                    stop=True), S["bcm"])
                p.wait("vector", S["bcm"])
                p.op("vector", lambda e: e.tensor_copy(mbc_f[:], ps_p[:]),
                     S["bce"])
                sl_pj.set_free(0, S["bce"])
                p.wait("tensor", S["rstd"])
                p.op("tensor", lambda e: e.matmul(
                    ps_l[:], ones_rf[:], srow2[0:1, :], start=True,
                    stop=True), S["bcm"])
                p.wait("vector", S["bcm"])
                p.op("vector", lambda e: e.tensor_copy(rbc_f[:], ps_l[:]),
                     S["bce"])
                sl_pj.set_free(1, S["bce"])

            def L5():
                p.wait("gpsimd", S["bce"])
                sl_xn.wait_free(p, "vector", hidx % 2)
                for eg in range(8):
                    sl = sls[eg]
                    p.op("vector", lambda e, sl=sl: e.tensor_tensor(
                        out=t1f[:], in0=xt[sl][:], in1=mbc_f[:],
                        op=ALU.subtract))
                    p.op("vector", lambda e, eg=eg, xns=xns: e.tensor_tensor(
                        out=xns[:, eg * 512:(eg + 1) * 512], in0=t1f[:],
                        in1=rbc_f[:], op=ALU.mult), S["app"])
                    sl_xt.set_free(sl, S["app"])
                sl_xb.set_free(hidx % 2, S["app"])
                app_cell[0] = S["app"].n
                p.wait("gpsimd", S["app"])
                for eg in range(8):
                    p.op("gpsimd", lambda e, eg=eg, c=c, half=half, xns=xns:
                         e.dma_start(
                             out=xn_img[eg * 128:(eg + 1) * 128,
                                        c * 1024 + half * 512:
                                        c * 1024 + half * 512 + 512],
                             in_=xns[:, eg * 512:(eg + 1) * 512]),
                         S["xns"], 16)

            return [L0, L1, L2, L3, L4, L5], app_cell

        def emit_xn_load(c, half, hidx, xns):
            p.wait("sync", S["xns"])
            for eg in range(8):
                p.op("sync", lambda e, eg=eg, c=c, half=half, xns=xns: e.dma_start(
                    out=xns[:, eg * 512:(eg + 1) * 512],
                    in_=xn_img[eg * 128:(eg + 1) * 128,
                               c * 1024 + half * 512: c * 1024 + half * 512 + 512]),
                    S["xnl"], 16)

        def emit_ln_proj(pp, t, c, att, zip_q=True):
            own = c >= 6
            q_fillers = []
            do_ln = c not in xn_done
            if do_ln:
                xn_done.add(c)
            for half in range(2):
                hidx = 2 * t + half
                xns = xn16[hidx % 2]
                ln_fills, app_cell = [], None
                if do_ln:
                    ln_fills, app_cell = emit_ln_fillers(t, c, half, hidx, xns)
                elif half == 0:
                    for h2 in range(2):
                        sl_xn.wait_free(p, "sync", (hidx + h2) % 2)
                        emit_xn_load(c, h2, hidx + h2, xn16[(hidx + h2) % 2])
                kv_buf = c % 2
                if half == 0:
                    sl_KV.wait_free(p, "tensor", kv_buf)
                xnl_n = S["xnl"].n if not do_ln else None
                grp = 0

                def dep_wait(app_cell=app_cell):
                    if app_cell is not None:
                        p.wait("tensor", S["app"], app_cell[0])
                    if xnl_n is not None:
                        p.wait("tensor", S["xnl"], xnl_n)

                fillers = list(ln_fills)

                def mk_kproj(mg, xns=xns, kv_buf=kv_buf, half=half,
                             wg=None):
                    wg = wld_goal[0]
                    def fn_a():
                        pj = [ps_p, ps_l][mg % 2]
                        p.wait("tensor", S["wld"], wg)
                        dep_wait()
                        sl_pj.wait_free(p, "tensor", mg % 2)
                        sl_kq.wait_free(p, "tensor", mg)
                        for eg in range(4):
                            p.op("tensor", lambda e, eg=eg, mg=mg, xns=xns, pj=pj:
                                 e.matmul(
                                     pj[:],
                                     wkv_sb[:, eg * 512 + mg * 128: eg * 512 + mg * 128 + 128],
                                     xns[:, eg * 512:(eg + 1) * 512],
                                     start=(eg == 0), stop=False), None)
                    def fn():
                        stg = kqst[mg]
                        pj = [ps_p, ps_l][mg % 2]
                        for eg in range(4, 8):
                            p.op("tensor", lambda e, eg=eg, mg=mg, xns=xns, pj=pj:
                                 e.matmul(
                                     pj[:],
                                     wkv_sb[:, eg * 512 + mg * 128: eg * 512 + mg * 128 + 128],
                                     xns[:, eg * 512:(eg + 1) * 512],
                                     start=False, stop=(eg == 7)),
                                 S["prj"] if eg == 7 else None)
                        p.wait("vector", S["prj"])
                        p.op("vector", lambda e, stg=stg, pj=pj: e.tensor_copy(
                            stg[:], pj[:]), S["pev"])
                        sl_pj.set_free(mg % 2, S["pev"])
                        p.wait("gpsimd", S["pev"])
                        skr = S[f"krs{kv_buf}"]
                        for hh in range(2):
                            hloc = 2 * mg + hh
                            p.op("gpsimd", lambda e, stg=stg, hh=hh, hloc=hloc,
                                 half=half, kv_buf=kv_buf: e.dma_start(
                                     out=K_aug[kv_buf][0:64,
                                                       hloc * 1024 + half * 512:
                                                       hloc * 1024 + half * 512 + 512],
                                     in_=stg[64 * hh:64 * hh + 64, :]), skr, 16)
                        sl_kq.set_free(mg, skr)
                    return [fn_a, fn]

                def mk_vproj(tt, xns=xns, kv_buf=kv_buf, half=half):
                    wg = wld_goal[0]
                    def fn_a():
                        pj = [ps_p, ps_l][tt % 2]
                        p.wait("tensor", S["wld"], wg)
                        dep_wait()
                        sl_pj.wait_free(p, "tensor", tt % 2)
                        for eg in range(4):
                            p.op("tensor", lambda e, tt=tt, eg=eg, xns=xns, pj=pj:
                                 e.matmul(
                                     pj[:, 0:256],
                                     xns[:, eg * 512 + tt * 128: eg * 512 + tt * 128 + 128],
                                     wkv_sb[:, eg * 512 + 256: eg * 512 + 512],
                                     start=(eg == 0), stop=False), None)
                    def fn():
                        ktl = 4 * half + tt
                        pj = [ps_p, ps_l][tt % 2]
                        for eg in range(4, 8):
                            p.op("tensor", lambda e, tt=tt, eg=eg, xns=xns, pj=pj:
                                 e.matmul(
                                     pj[:, 0:256],
                                     xns[:, eg * 512 + tt * 128: eg * 512 + tt * 128 + 128],
                                     wkv_sb[:, eg * 512 + 256: eg * 512 + 512],
                                     start=False, stop=(eg == 7)),
                                 S["prj"] if eg == 7 else None)
                        p.wait("vector", S["prj"])
                        p.op("vector", lambda e, ktl=ktl, kv_buf=kv_buf, pj=pj:
                             e.tensor_copy(
                                 V_aug[kv_buf][:, ktl * 260: ktl * 260 + 260]
                                 .rearrange("p (h c) -> p h c", c=65)[:, :, 0:64],
                                 pj[:, 0:256].rearrange("p (h c) -> p h c", c=64)),
                             S["pev"])
                        sl_pj.set_free(tt % 2, S["pev"])
                    return [fn_a, fn]

                def mk_qproj(mg, xns=xns, c=c, half=half):
                    wg = wld_goal[0]
                    def fn_a():
                        pj = [ps_p, ps_l][mg % 2]
                        p.wait("tensor", S["wld"], wg)
                        dep_wait()
                        sl_pj.wait_free(p, "tensor", mg % 2)
                        sl_kq.wait_free(p, "tensor", 2 + mg)
                        for eg in range(4):
                            p.op("tensor", lambda e, mg=mg, eg=eg, xns=xns, pj=pj:
                                 e.matmul(
                                     pj[:],
                                     wq_sb[:, eg * 256 + mg * 128: eg * 256 + mg * 128 + 128],
                                     xns[:, eg * 512:(eg + 1) * 512],
                                     start=(eg == 0), stop=False), None)
                    def fn():
                        stg = kqst[2 + mg]
                        pj = [ps_p, ps_l][mg % 2]
                        for eg in range(4, 8):
                            p.op("tensor", lambda e, mg=mg, eg=eg, xns=xns, pj=pj:
                                 e.matmul(
                                     pj[:],
                                     wq_sb[:, eg * 256 + mg * 128: eg * 256 + mg * 128 + 128],
                                     xns[:, eg * 512:(eg + 1) * 512],
                                     start=False, stop=(eg == 7)),
                                 S["prj"] if eg == 7 else None)
                        p.wait("vector", S["prj"])
                        p.op("vector", lambda e, stg=stg, pj=pj: e.tensor_scalar_mul(
                            out=stg[:], in0=pj[:], scalar1=0.125), S["pev"])
                        sl_pj.set_free(mg % 2, S["pev"])
                        p.wait("gpsimd", S["pev"])
                        qcol = (c - 6) * 1024 + half * 512
                        sqr = S[f"qrs{c}"]
                        for hh in range(2):
                            hloc = 2 * mg + hh
                            p.op("gpsimd", lambda e, stg=stg, hh=hh, hloc=hloc,
                                 qcol=qcol: e.dma_start(
                                     out=Q_aug[0:64, hloc * 2048 + qcol:
                                               hloc * 2048 + qcol + 512],
                                     in_=stg[64 * hh:64 * hh + 64, :]), sqr, 16)
                        sl_kq.set_free(2 + mg, sqr)
                    return [fn_a, fn]

                for mg in range(2):
                    fillers.extend(mk_kproj(mg))
                for tt in range(4):
                    fillers.extend(mk_vproj(tt))
                if own:
                    for mg in range(2):
                        (fillers if zip_q else q_fillers).extend(mk_qproj(mg))

                def post_fn(hidx=hidx, do_ln=do_ln):
                    if do_ln:
                        sl_xn.set_free(hidx % 2, S["prj"], S["xns"])
                    else:
                        sl_xn.set_free(hidx % 2, S["prj"])
                (fillers if zip_q else q_fillers).append(post_fn)

                if att is not None:
                    emit_attention(att[0], att[1], att[2],
                                   [2 * half, 2 * half + 1], fillers)
                for fn in fillers:
                    fn()
                fillers.clear()
                for fn in q_fillers:
                    qproj_pending.append(fn)
                q_fillers = []
            kv_buf = c % 2
            sl_KV0.wait_free(p, "sync", kv_buf)
            sl_KV.wait_free(p, "sync", kv_buf)
            p.op("sync", lambda e, pp=pp, c=c, kv_buf=kv_buf: e.dma_start(
                out=K_aug[kv_buf][65:68, :], in_=ka_d[pp, c]), S[f"kb{kv_buf}"], 16)
            return {"krs": S[f"krs{kv_buf}"].n, "pev": S["pev"].n,
                    "kb": S[f"kb{kv_buf}"].n, "qrs6": S["qrs6"].n,
                    "qrs7": S["qrs7"].n}

        pair_i = [0]
        pending = []

        def emit_attention(pp, c, snap, heads, fillers=None):
            kv_buf = c % 2
            fill_ctr = [0]
            p.wait("tensor", S[f"krs{kv_buf}"], snap["krs"])
            p.wait("tensor", S[f"kb{kv_buf}"], snap["kb"])
            p.wait("tensor", S["pev"], snap["pev"])
            p.wait("tensor", S["qrs6"], snap["qrs6"])
            p.wait("tensor", S["qrs7"], snap["qrs7"])
            p.wait("tensor", S["qau"])
            p.wait("tensor", S["ini"])
            p.wait("vector", S["wld"], 80)
            for hl in heads:
                hg = pp * HPP + hl
                for m in range(4):
                    q0 = Q0 + 512 * m
                    hi_kt = min(8 * c + 7, (q0 + 511) // 128)
                    lo_kt = max(8 * c, band_lo_kt(hg, q0))
                    if lo_kt > hi_kt:
                        continue
                    kts = list(range(lo_kt, hi_kt + 1))
                    cps = ps_c[m % 2]
                    def su_live(su, kt, q0=q0, hg=hg):
                        squ0 = q0 + su * 128
                        if kt * 128 > squ0 + 127:
                            return False
                        if SLOPES[hg] * (squ0 - (kt * 128 + 127)) >= BAND:
                            return False
                        return True
                    live_mms = [(kt, su) for kt in kts for su in range(4)
                                if su_live(su, kt)]
                    if not live_mms:
                        continue
                    n_mm = len(live_mms)
                    su_set = sorted({su for _, su in live_mms})
                    mm_i = 0
                    pairs = [kts[i:i + 2] for i in range(0, len(kts), 2)]
                    npairs = len(pairs)
                    for pidx, pr in enumerate(pairs):
                        fill_ctr[0] += 1
                        if fillers and fill_ctr[0] >= 2:
                            fillers.pop(0)()
                        sb = pair_i[0] % 2
                        ss = ps_s[sb]
                        sl_ss.wait_free(p, "tensor", sb)
                        offs = []
                        wrs = []
                        for j, kt in enumerate(pr):
                            ktl = kt - 8 * c
                            dj = kt - q0 // 128
                            off = 128 * dj if dj > 0 else 0
                            wr = 128 * (1 + max(su for su in range(4)
                                                if su_live(su, kt)))
                            offs.append(off)
                            wrs.append(wr)
                            p.op("tensor", lambda e, ss=ss, j=j, hl=hl, ktl=ktl,
                                 m=m, kv_buf=kv_buf, off=off, wr=wr: e.matmul(
                                     ss[:, 512 * j + off:512 * j + wr],
                                     K_aug[kv_buf][:, hl * 1024 + ktl * 128:
                                                   hl * 1024 + ktl * 128 + 128],
                                     Q_aug[:, hl * 2048 + m * 512 + off:
                                           hl * 2048 + m * 512 + wr],
                                     start=True, stop=True), S["sc"])
                        nmask = 0
                        for j, kt in enumerate(pr):
                            dj = kt - q0 // 128
                            if dj >= 0:
                                p.wait("vector", S["sc"])
                                p.op("vector", lambda e, ss=ss, j=j, dj=dj:
                                     e.tensor_tensor(
                                         out=ss[:, 512 * j + 128 * dj:
                                                512 * j + 128 * dj + 128],
                                         in0=ss[:, 512 * j + 128 * dj:
                                                512 * j + 128 * dj + 128],
                                         in1=masks[dj][:, 128 * dj:128 * dj + 128],
                                         op=ALU.add), S["msk"])
                                nmask += 1
                        pts = PT[pair_i[0] % 6]
                        sl_PT.wait_free(p, "scalar", pair_i[0] % 6)
                        p.wait("scalar", S["sc"])
                        if nmask:
                            p.wait("scalar", S["msk"])
                        w = 512 * (len(pr) - 1) + wrs[-1]
                        o0 = offs[0]
                        p.op("scalar", lambda e, pts=pts, ss=ss, w=w, o0=o0:
                             e.activation(out=pts[:, o0:w], in_=ss[:, o0:w],
                                          func=AF.Exp), S["exp"])
                        sl_ss.set_free(sb, S["exp"])

                        mm_specs = []
                        for j, kt in enumerate(pr):
                            for su in range(4):
                                if not su_live(su, kt):
                                    continue
                                mm_specs.append((j, kt, su, mm_i == 0,
                                                 mm_i == n_mm - 1))
                                mm_i += 1
                        is_last_pair = (pidx == npairs - 1)
                        is_first_pair = (pidx == 0)
                        my_slot = pair_i[0] % 6
                        exp_n = S["exp"].n

                        def ctx_fn(mm_specs=mm_specs, cps=cps, kv_buf=kv_buf,
                                   hl=hl, m=m, c=c, pts=pts, my_slot=my_slot,
                                   exp_n=exp_n, is_last_pair=is_last_pair,
                                   is_first_pair=is_first_pair,
                                   su_set=su_set, q0=q0):
                            if is_first_pair:
                                sl_cs.wait_free(p, "tensor", m % 2)
                            p.wait("tensor", S["exp"], exp_n)
                            for j, kt, su, first, last in mm_specs:
                                p.op("tensor", lambda e, cps=cps, su=su, j=j,
                                     kt=kt, c=c, kv_buf=kv_buf, hl=hl, pts=pts,
                                     first=first, last=last: e.matmul(
                                         cps[:, su * 65: su * 65 + 65],
                                         pts[:, 512 * j + su * 128:
                                             512 * j + su * 128 + 128],
                                         V_aug[kv_buf][:, (kt - 8 * c) * 260 + hl * 65:
                                                       (kt - 8 * c) * 260 + hl * 65 + 65],
                                         start=first, stop=last), S["ctx"])
                            sl_PT.set_free(my_slot, S["ctx"])
                            if is_last_pair:
                                p.op("tensor", lambda e: e.nop(), S["ctx"])
                                p.wait("vector", S["ctx"])
                                first_ever = (c == (q0 + 511) // 1024)
                                for su in range(su_set[0], su_set[-1] + 1):
                                    acc = ctx_acc(hl, 4 * m + su)
                                    if first_ever:
                                        p.op("vector", lambda e, acc=acc, cps=cps,
                                             su=su: e.tensor_copy(
                                                 acc, cps[:, su * 65:su * 65 + 65]),
                                             S["cev"])
                                    else:
                                        p.op("vector", lambda e, acc=acc, cps=cps,
                                             su=su: e.tensor_tensor(
                                                 out=acc,
                                                 in0=cps[:, su * 65:su * 65 + 65],
                                                 in1=acc, op=ALU.add), S["cev"])
                                sl_cs.set_free(m % 2, S["cev"])

                        pending.append(ctx_fn)
                        if len(pending) > 5:
                            pending.pop(0)()
                        pair_i[0] += 1
            if heads[-1] == 3:
                while pending:
                    pending.pop(0)()
                p.op("tensor", lambda e: e.nop(), S["sc"])
                sl_KV.set_free(kv_buf, S["sc"])

        def emit_finalize(pp, zipper=None):
            p.wait("vector", S["ctx"])
            p.wait("scalar", S["cev"])
            nrm_when = {}
            prev_tp = [None]
            zi = [0]
            for su in range(16):
                if zipper:
                    while zi[0] < (su * 3) // 2 and zipper:
                        zipper.pop(0)()
                        zi[0] += 1
                if su >= 4:
                    p.wait("vector", S["nrm"], nrm_when[su - 4])
                for hl in range(HPP):
                    acc = ctx_acc(hl, su)
                    sl = (su * HPP + hl) % 16
                    p.op("vector", lambda e, acc=acc, sl=sl: e.reciprocal(
                        rec[:, sl:sl + 1], acc[:, 64:65]), S["recs"])
                cn = ctxn[su % 2]
                sl_ctxn.wait_free(p, "scalar", su % 2)
                p.wait("scalar", S["recs"])
                for hl in range(HPP):
                    acc = ctx_acc(hl, su)
                    sl = (su * HPP + hl) % 16
                    if hl < 2:
                        p.op("scalar", lambda e, acc=acc, cn=cn, hl=hl, sl=sl:
                             e.activation(out=cn[:, hl * 64:(hl + 1) * 64],
                                          in_=acc[:, 0:64], func=AF.Copy,
                                          scale=rec[:, sl:sl + 1]), S["nrm"])
                    else:
                        p.op("vector", lambda e, acc=acc, cn=cn, hl=hl, sl=sl:
                             e.tensor_scalar_mul(
                                 out=cn[:, hl * 64:(hl + 1) * 64],
                                 in0=acc[:, 0:64],
                                 scalar1=rec[:, sl:sl + 1]), S["nrm"])
                nrm_when[su] = S["nrm"].n

                def tp_fn(su=su, cn=cn, pp=pp, nrm_n=S["nrm"].n):
                    p.wait("tensor", S["nrm"], nrm_n)
                    for hp in range(2):
                        cslot = (su * 2 + hp) % 8
                        ct = ctt[cslot]
                        tb, trk, tix = [(ps_p, sl_pj, 0), (ps_l, sl_pj, 1),
                                        (ps_c[0], sl_cs, 0),
                                        (ps_c[1], sl_cs, 1)][cslot % 4]
                        tbv = tb.bitcast(BF16)[:, 0:128]
                        sl_ctt.wait_free(p, "tensor", cslot)
                        trk.wait_free(p, "tensor", tix)
                        p.op("tensor", lambda e, cn=cn, hp=hp, tbv=tbv:
                             e.transpose(tbv, cn[:, hp * 128:(hp + 1) * 128],
                                         ident[:]), S["tp"])
                        p.wait("vector", S["tp"])
                        p.op("vector", lambda e, ct=ct, tbv=tbv: e.tensor_copy(
                            ct[:], tbv), S["tpe"])
                        trk.set_free(tix, S["tpe"])
                        p.wait("gpsimd", S["tpe"])
                        p.op("gpsimd", lambda e, ct=ct, pp=pp, hp=hp, su=su:
                             e.dma_start(
                                 out=ctxT_img[pp * 256 + hp * 128:
                                              pp * 256 + (hp + 1) * 128,
                                              su * 128:(su + 1) * 128],
                                 in_=ct[:]), S["cts"], 16)
                        sl_ctt.set_free(cslot, S["cts"])
                    sl_ctxn.set_free(su % 2, S["tp"])

                if prev_tp[0] is not None:
                    prev_tp[0]()
                prev_tp[0] = tp_fn
            prev_tp[0]()


        carry = [None]
        qproj_pending = []
        def emit_wload(pp):
            if pp == 0:
                for wi in range(8):
                    p.op("gpsimd", lambda e, wi=wi: e.dma_start(
                        out=wkv_sb[:, wi * 512:(wi + 1) * 512],
                        in_=wkv_d[0, wi * 128:(wi + 1) * 128, :]),
                        S["wld"], 16)
                p.op("gpsimd", lambda e: e.dma_start(
                    out=wq_sb[:].rearrange("p (w c) -> p w c", c=256),
                    in_=wq_d[0].rearrange("(w p) c -> p w c", p=128)),
                    S["wld"], 16)
                return
            p.wait("sync", S["wld"])
            p.op("sync", lambda e, pp=pp: e.dma_start(
                out=xtq[:].rearrange("p (w c) -> p w c", c=256),
                in_=wq_d[pp].rearrange("(w p) c -> p w c", p=128)),
                S["qld"], 16)
            for wi in range(8):
                sl_xt.wait_free(p, "sync", wi)
                p.op("sync", lambda e, pp=pp, wi=wi: e.dma_start(
                    out=xt[wi][:],
                    in_=wkv_d[pp, wi * 128:(wi + 1) * 128, :]),
                    S[f"xld{wi}"], 16)

        def emit_wcast(pp):
            if pp == 0:
                return
            p.wait("vector", S["prj"])
            for wi in range(8):
                p.wait("vector", S[f"xld{wi}"])
                p.op("vector", lambda e, wi=wi: e.tensor_copy(
                    wkv_sb[:, wi * 512:(wi + 1) * 512], xt[wi][:]),
                    S["wld"])
                sl_xt.set_free(wi, S["wld"])
            p.wait("vector", S["qld"])
            for wi in range(8):
                p.op("vector", lambda e, wi=wi: e.tensor_copy(
                    wq_sb[:, wi * 256:(wi + 1) * 256],
                    xtq[:, wi * 256:wi * 256 + 256]), S["wld"])

        wld_goal = [0]
        for pp in range(NPASS):
            if pp == 0:
                emit_wload(0)
            emit_wcast(pp)
            wld_goal[0] = S["wld"].n
            p.wait("gpsimd", S["wld"])

            snaps = {}
            lo_c = pass_lo_chunk(pp)
            for t, c in enumerate(range(7, lo_c - 1, -1)):
                att = carry[0] if t == 0 else (pp, c + 1, snaps[c + 1])
                snaps[c] = emit_ln_proj(pp, t, c, att, zip_q=(t != 0))
                if t == 0:
                    p.wait("sync", S["sc"])
                    p.wait("sync", S["ini"])
                    for hl in range(HPP):
                        p.op("sync", lambda e, hl=hl, pp=pp: e.dma_start(
                            out=Q_aug[64:68, hl * 2048:(hl + 1) * 2048],
                            in_=qaug_d[pp * HPP + hl]), S["qau"], 16)
                    p.wait("gpsimd", S["sc"])
                    while qproj_pending:
                        qproj_pending.pop(0)()
                    snaps[7]["qrs6"] = S["qrs6"].n
                    snaps[7]["qrs7"] = S["qrs7"].n
                    snaps[7]["pev"] = S["pev"].n
                    if pp > 0:
                        emit_finalize(pp - 1)
                    p.wait("vector", S["nrm"])
            carry[0] = (pp, lo_c, snaps[lo_c])
            if pp + 1 < NPASS:
                emit_wload(pp + 1)

        emit_attention(carry[0][0], carry[0][1], carry[0][2],
                       [0, 1, 2, 3], None)

        AB.release(); AFA.release()
        AB.mark(); AFA.mark()
        h_sb = AB.alloc(24 * 1024)
        xnT_sb = AB.alloc(8192)
        wg_bf = [AB.alloc(1024) for _ in range(3)]
        wu_bf = [AB.alloc(1024) for _ in range(3)]
        wd_bf = [AB.alloc(2048) for _ in range(3)]
        sg_bf = [AB.alloc(512) for _ in range(2)]
        ctxw = [AB.alloc(1024) for _ in range(2)]
        woT_sb = xnT_sb
        gst = [AFA.alloc(512) for _ in range(4)]
        xos = [AFA.alloc(512) for _ in range(4)]
        sld = [AFA.alloc(512) for _ in range(4)]
        outs = [AFA.alloc(512) for _ in range(4)]
        sst = [gst[0], gst[1], sld[0], sld[1]]
        wg_i = [AFA.alloc(256).bitcast(I8) for _ in range(2)]
        wu_i = [AFA.alloc(256).bitcast(I8) for _ in range(2)]
        wd_i = [AFA.alloc(512).bitcast(I8) for _ in range(3)]

        sl_wgi = Slots(2)
        sl_wui = Slots(2)
        sl_wdi = Slots(3)
        sl_wgb = Slots(3)
        sl_wub = Slots(3)
        sl_wdb = Slots(3)
        sl_sg = Slots(2)
        sl_gst = Slots(4)
        sl_xo = Slots(4)
        sl_out = Slots(4)
        sl_sld = Slots(4)
        sl_sst = Slots(4)
        sl_ctw = Slots(2)
        sl_dps = Slots(8)

        down_banks = [ps_s[0][:, 0:512], ps_s[0][:, 512:1024],
                      ps_s[1][:, 0:512], ps_s[1][:, 512:1024],
                      ps_c[0][:], ps_c[1][:], ps_p[:], ps_l[:]]

        def ffn_load_deq(hc):
            gslot = hc % 3
            islot = hc % 2
            sl_wgi.wait_free(p, "sync", islot)
            p.op("sync", lambda e, hc=hc, islot=islot: e.dma_start(
                out=wg_i[islot][:], in_=wg_d[hc]), S[f"wgl{islot}"], 16)
            sl_wui.wait_free(p, "sync", islot)
            p.op("sync", lambda e, hc=hc, islot=islot: e.dma_start(
                out=wu_i[islot][:], in_=wu_d[hc]), S[f"wul{islot}"], 16)
            p.wait("scalar", S[f"wgl{islot}"])
            p.wait("scalar", S[f"wul{islot}"])
            sl_wgb.wait_free(p, "scalar", gslot)
            p.op("scalar", lambda e, gslot=gslot, islot=islot: e.activation(
                out=wg_bf[gslot][:], in_=wg_i[islot][:], func=AF.Copy),
                S["dqw"])
            sl_wgi.set_free(islot, S["dqw"])
            sl_wub.wait_free(p, "scalar", gslot)
            p.op("scalar", lambda e, gslot=gslot, islot=islot: e.activation(
                out=wu_bf[gslot][:], in_=wu_i[islot][:], func=AF.Copy),
                S["dqw"])
            sl_wui.set_free(islot, S["dqw"])

        def mk_gate_unit(tg, hc):
            def fn():
                gslot = hc % 3
                if hc + 2 < 24:
                    ffn_load_deq(hc + 2)
                p.wait("tensor", S["dqw"],
                       S["dqw"].n - (2 if hc + 2 < 24 else 0))
                p.wait("tensor", S["fxl"])
                for nh in range(2):
                    gb = down_banks[nh]
                    ub = down_banks[2 + nh]
                    sl_dps.wait_free(p, "tensor", nh)
                    sl_dps.wait_free(p, "tensor", 2 + nh)
                    for eg in range(8):
                        p.op("tensor", lambda e, gb=gb, eg=eg, nh=nh, gslot=gslot:
                             e.matmul(gb, wg_bf[gslot][:, eg * 128:(eg + 1) * 128],
                                      xnT_sb[:, eg * 1024 + nh * 512:
                                             eg * 1024 + nh * 512 + 512],
                                      start=(eg == 0), stop=(eg == 7)),
                             S["gmm"] if eg == 7 else None)
                    for eg in range(8):
                        p.op("tensor", lambda e, ub=ub, eg=eg, nh=nh, gslot=gslot:
                             e.matmul(ub, wu_bf[gslot][:, eg * 128:(eg + 1) * 128],
                                      xnT_sb[:, eg * 1024 + nh * 512:
                                             eg * 1024 + nh * 512 + 512],
                                      start=(eg == 0), stop=(eg == 7)),
                             S["gmm"] if eg == 7 else None)
                    sgs = sg_bf[nh]
                    sl_sg.wait_free(p, "scalar", nh)
                    p.wait("scalar", S["gmm"], S["gmm"].n - 1)
                    p.op("scalar", lambda e, gb=gb, sgs=sgs: e.activation(
                        out=sgs[:], in_=gb, func=AF.Silu, scale=DEQ), S["sgl"])
                    sl_dps.set_free(nh, S["sgl"])
                    p.wait("vector", S["gmm"])
                    p.wait("vector", S["sgl"])
                    p.op("vector", lambda e, ub=ub, sgs=sgs, hc=hc, nh=nh:
                         e.scalar_tensor_tensor(
                             out=h_sb[:, hc * 1024 + nh * 512:
                                      hc * 1024 + nh * 512 + 512],
                             in0=ub, scalar=DEQ, in1=sgs[:],
                             op0=ALU.mult, op1=ALU.mult), S["hml"])
                    sl_dps.set_free(2 + nh, S["hml"])
                    sl_sg.set_free(nh, S["hml"])
                sl_wgb.set_free(gslot, S["gmm"])
                sl_wub.set_free(gslot, S["gmm"])
            return fn

        def emit_gate_prologue(tg, preloaded=False):
            if tg > 0:
                p.wait("sync", S["gmm"])
                p.wait("sync", S["smm"])
            p.wait("sync", S["xns"])
            for eg in range(8):
                p.op("sync", lambda e, eg=eg, tg=tg: e.dma_start(
                    out=xnT_sb[:, eg * 1024:(eg + 1) * 1024],
                    in_=xn_img[eg * 128:(eg + 1) * 128,
                               Q0 + tg * 1024: Q0 + (tg + 1) * 1024]),
                    S["fxl"], 16)
            if not preloaded:
                ffn_load_deq(0)
                ffn_load_deq(1)

        p.wait("tensor", S["exp"])
        p.wait("tensor", S["cev"])
        for sem in ("sc", "ctx", "prj", "pev", "exp", "cev",
                    "krs0", "krs1", "qrs6", "qrs7", "app", "xnl"):
            p.wait("sync", S[sem])
        emit_gate_prologue(0)
        gate_units = [mk_gate_unit(0, hc) for hc in range(24)]
        emit_finalize(NPASS - 1, gate_units)
        while gate_units:
            gate_units.pop(0)()

        p.wait("gpsimd", S["cts"])
        p.wait("gpsimd", S["wld"])
        p.wait("gpsimd", S["gmm"])
        p.op("gpsimd", lambda e: e.dma_start(
            out=woT_sb[:].rearrange("p (a c) -> p a c", a=8),
            in_=wo_d.rearrange("(a p) c -> p a c", p=128)), S["wol"], 16)
        p.wait("tensor", S["wol"])
        p.wait("tensor", S["sgl"])
        p.wait("tensor", S["hml"])
        p.wait("sync", S["cts"])

        def ctw_load(su):
            cw = ctxw[su % 2]
            scw = S[f"ctw{su % 2}"]
            sl_ctw.wait_free(p, "sync", su % 2)
            for eg in range(8):
                p.op("sync", lambda e, eg=eg, su=su, cw=cw: e.dma_start(
                    out=cw[:, eg * 128:(eg + 1) * 128],
                    in_=ctxT_img[eg * 128:(eg + 1) * 128,
                                 su * 128:(su + 1) * 128]), scw, 16)
            return S[f"ctw{su % 2}"].n

        ctw_when = {0: ctw_load(0)}
        for su in range(16):
            cw = ctxw[su % 2]
            if su + 1 < 16:
                ctw_when[su + 1] = ctw_load(su + 1)
            spb = ps_s[su % 2]
            sl_ss.wait_free(p, "tensor", su % 2)
            p.wait("tensor", S[f"ctw{su % 2}"], ctw_when[su])
            for eh in range(2):
                for eg in range(8):
                    p.op("tensor", lambda e, spb=spb, cw=cw, eh=eh, eg=eg:
                         e.matmul(
                             spb[:, eh * 512:(eh + 1) * 512],
                             cw[:, eg * 128:(eg + 1) * 128],
                             woT_sb[:, eg * 1024 + eh * 512:
                                    eg * 1024 + eh * 512 + 512],
                             start=(eg == 0), stop=(eg == 7)),
                         S["smm"] if eg == 7 else None)
            sl_ctw.set_free(su % 2, S["smm"])
            p.wait("vector", S["smm"])
            for eh in range(2):
                sslot = (su * 2 + eh) % 4
                sl_sst.wait_free(p, "vector", sslot)
                p.op("vector", lambda e, spb=spb, eh=eh, sslot=sslot:
                     e.tensor_copy(sst[sslot][:],
                                   spb[:, eh * 512:(eh + 1) * 512]), S["sev"])
                p.wait("gpsimd", S["sev"])
                p.op("gpsimd", lambda e, su=su, eh=eh, sslot=sslot: e.dma_start(
                    out=s_img[su * 128:(su + 1) * 128, eh * 512:(eh + 1) * 512],
                    in_=sst[sslot][:]), S["sst"], 16)
                sl_sst.set_free(sslot, S["sst"])
            sl_ss.set_free(su % 2, S["sev"])

        def wd_load_deq(hcp):
            dslot = hcp % 3
            sl_wdi.wait_free(p, "sync", dslot)
            p.op("sync", lambda e, hcp=hcp, dslot=dslot: e.dma_start(
                out=wd_i[dslot][:], in_=wd_d[hcp]), S[f"wdl{dslot}"], 16)
            p.wait("scalar", S[f"wdl{dslot}"])
            sl_wdb.wait_free(p, "scalar", dslot)
            p.op("scalar", lambda e, dslot=dslot: e.activation(
                out=wd_bf[dslot][:], in_=wd_i[dslot][:], func=AF.Copy),
                S["dqw"])
            sl_wdi.set_free(dslot, S["dqw"])

        def emit_down(tg, preloaded=False):
            p.wait("tensor", S["hml"])
            p.wait("tensor", S["sev"])
            p.wait("scalar", S["sst"])
            for th in range(2):
                if th == 0 and not preloaded:
                    wd_load_deq(0)
                    wd_load_deq(1)
                for hcp in range(12):
                    dslot = hcp % 3
                    nxt = hcp + 2
                    if nxt < 12:
                        wd_load_deq(nxt)
                    elif th == 0:
                        wd_load_deq(nxt - 12)
                    p.wait("tensor", S["dqw"],
                           S["dqw"].n - (2 if (nxt < 12 or th == 0) else
                                         (1 if hcp < 11 else 0)))
                    for sub in range(2):
                        hc = 2 * hcp + sub
                        for bi in range(8):
                            eh, t4 = bi // 4, bi % 4
                            tt = th * 4 + t4
                            bk = down_banks[bi]
                            if hc == 0:
                                sl_dps.wait_free(p, "tensor", bi)
                            p.op("tensor", lambda e, bk=bk, hc=hc, tt=tt, eh=eh,
                                 dslot=dslot, sub=sub: e.matmul(
                                     bk,
                                     h_sb[:, hc * 1024 + tt * 128:
                                          hc * 1024 + tt * 128 + 128],
                                     wd_bf[dslot][:, sub * 1024 + eh * 512:
                                                  sub * 1024 + eh * 512 + 512],
                                     start=(hc == 0), stop=(hc == 23)),
                                 S["dmm"] if bi == 7 else None)
                    sl_wdb.set_free(dslot, S["dmm"])
                def tail_load(j, tg=tg, th=th):
                    gslot = j % 4
                    eh, t4 = j // 4, j % 4
                    tt = th * 4 + t4
                    sl_xo.wait_free(p, "sync", gslot)
                    p.op("sync", lambda e, tg=tg, tt=tt, gslot=gslot, eh=eh:
                         e.dma_start(
                             out=xos[gslot][:],
                             in_=xo_d[tg * 1024 + tt * 128:
                                      tg * 1024 + tt * 128 + 128,
                                      eh * 512:(eh + 1) * 512]),
                         S[f"xol{gslot}"], 16)
                    p.wait("sync", S["sst"])
                    sl_sld.wait_free(p, "sync", gslot)
                    p.op("sync", lambda e, tg=tg, tt=tt, gslot=gslot, eh=eh:
                         e.dma_start(
                             out=sld[gslot][:],
                             in_=s_img[tg * 1024 + tt * 128:
                                       tg * 1024 + tt * 128 + 128,
                                       eh * 512:(eh + 1) * 512]),
                         S[f"sld{gslot}"], 16)

                for j in range(4):
                    tail_load(j)
                for bi in range(8):
                    eh, t4 = bi // 4, bi % 4
                    tt = th * 4 + t4
                    bk = down_banks[bi]
                    gslot = bi % 4
                    sl_gst.wait_free(p, "scalar", gslot)
                    p.wait("scalar", S["dmm"])
                    p.op("scalar", lambda e, bk=bk, gslot=gslot: e.activation(
                        out=gst[gslot][:], in_=bk, func=AF.Silu, scale=DEQ),
                        S["gsl"])
                    sl_dps.set_free(bi, S["gsl"])
                    sl_out.wait_free(p, "vector", gslot)
                    p.wait("vector", S["gsl"])
                    p.wait("vector", S[f"xol{gslot}"])
                    p.wait("vector", S[f"sld{gslot}"])
                    p.op("vector", lambda e, gslot=gslot: e.tensor_tensor(
                        out=outs[gslot][:], in0=gst[gslot][:],
                        in1=sld[gslot][:], op=ALU.mult))
                    p.op("vector", lambda e, gslot=gslot: e.tensor_tensor(
                        out=outs[gslot][:], in0=outs[gslot][:],
                        in1=xos[gslot][:], op=ALU.add), S["cmb"])
                    sl_gst.set_free(gslot, S["cmb"])
                    sl_xo.set_free(gslot, S["cmb"])
                    sl_sld.set_free(gslot, S["cmb"])
                    p.wait("gpsimd", S["cmb"])
                    p.op("gpsimd", lambda e, gslot=gslot, tg=tg, tt=tt, eh=eh:
                         e.dma_start(
                             out=y_d[tg * 1024 + tt * 128:
                                     tg * 1024 + tt * 128 + 128,
                                     eh * 512:(eh + 1) * 512],
                             in_=outs[gslot][:]), S["yst"], 16)
                    sl_out.set_free(gslot, S["yst"])
                    if bi + 4 < 8:
                        tail_load(bi + 4)

        wd_load_deq(0)
        wd_load_deq(1)
        p.wait("sync", S["gmm"])
        ffn_load_deq(0)
        ffn_load_deq(1)
        emit_down(0, preloaded=True)
        emit_gate_prologue(1, preloaded=True)
        for hc in range(24):
            mk_gate_unit(1, hc)()
            if hc == 16:
                wd_load_deq(0)
                wd_load_deq(1)
        emit_down(1, preloaded=True)
        AB.release(); AFA.release()

        p.wait("gpsimd", S["yst"])
        p.verify()

        for s in S.values():
            s.handle = ent(nc.semaphore("sem_" + s.name))
        with nc.Block() as block:
            p.emit(block)
    return nc


def split3(v):
    bf16 = ml_dtypes.bfloat16
    a1 = v.astype(np.float32).astype(bf16).astype(np.float64)
    a2 = (v - a1).astype(np.float32).astype(bf16).astype(np.float64)
    a3 = (v - a1 - a2).astype(np.float32).astype(bf16)
    return a1.astype(bf16), a2.astype(bf16), a3


def host_constants():
    bf16 = ml_dtypes.bfloat16
    q = np.arange(Q0, WIN, dtype=np.float64)
    qaug = np.zeros((H, 4, QOWN), dtype=bf16)
    for h in range(H):
        qaug[h, 0] = -((SLOPES[h] * q).astype(np.float32).astype(bf16))
        qaug[h, 1:4] = 1.0
    masks = np.zeros((4, 128, 512), np.float32)
    for j in range(4):
        for sc in range(4):
            blk = masks[j][:, sc * 128:(sc + 1) * 128]
            if sc < j:
                blk[:] = NEG
            elif sc == j:
                kk = np.arange(128)[:, None]
                qq = np.arange(128)[None, :]
                blk[:] = np.where(kk > qq, NEG, 0.0)
    ident = np.eye(128, dtype=bf16)
    return qaug, masks, ident


def kernel(L0_attn_x, L0_attn_w_in, L0_attn_w_out, L0_ffn_w_gate, L0_ffn_w_up,
           L0_ffn_w_down):
    x = np.ascontiguousarray(np.asarray(L0_attn_x, dtype=np.float32))
    w_in = np.asarray(L0_attn_w_in, dtype=np.float32)
    w_out = np.asarray(L0_attn_w_out, dtype=np.float32)
    def i8_lowbyte(a):
        a = np.ascontiguousarray(np.asarray(a, dtype="<i4"))
        return a.view(np.int8).reshape(a.shape + (4,))[..., 0]

    w_g = i8_lowbyte(np.asarray(L0_ffn_w_gate, dtype=np.int32).T)
    w_u = i8_lowbyte(np.asarray(L0_ffn_w_up, dtype=np.int32).T)
    w_d = i8_lowbyte(np.asarray(L0_ffn_w_down, dtype=np.int32).T)
    w_g = np.ascontiguousarray(
        w_g.reshape(8, 128, 24, 128).transpose(2, 1, 0, 3).reshape(24, 128, 1024))
    w_u = np.ascontiguousarray(
        w_u.reshape(8, 128, 24, 128).transpose(2, 1, 0, 3).reshape(24, 128, 1024))
    w_d = np.ascontiguousarray(
        w_d.reshape(12, 2, 128, 1024).transpose(0, 2, 1, 3).reshape(12, 128, 2048))

    qaug, masks, ident = host_constants()

    wq = w_in[0:E]
    wk = w_in[E:2 * E]
    wv = w_in[2 * E:3 * E]
    wkvp = np.zeros((NPASS, E, 512), np.float32)
    wqp = np.zeros((NPASS, E, 256), np.float32)
    for pp in range(NPASS):
        hs = slice(pp * 256, pp * 256 + 256)
        wkvp[pp, :, 0:256] = wk[hs].T
        wkvp[pp, :, 256:512] = wv[hs].T
        wqp[pp] = wq[hs].T
    wop = np.ascontiguousarray(w_out.T)

    bf16 = ml_dtypes.bfloat16
    ka_base = np.zeros((NPASS, 8, 3, HPP * 1024), dtype=bf16)
    kw = np.arange(WIN, dtype=np.float64).reshape(8, 1024)
    for pp in range(NPASS):
        for hl in range(HPP):
            sl = SLOPES[pp * HPP + hl]
            for c in range(8):
                a1, a2, a3 = split3(sl * kw[c])
                ka_base[pp, c, 0, hl * 1024:(hl + 1) * 1024] = a1
                ka_base[pp, c, 1, hl * 1024:(hl + 1) * 1024] = a2
                ka_base[pp, c, 2, hl * 1024:(hl + 1) * 1024] = a3

    nc = build_nc()
    in_maps = []
    for core in range(NCORES):
        b, qq = core // 4, core % 4
        q_end = (qq + 1) * QOWN
        pad = WIN - q_end
        xT_win = np.zeros((E, WIN), np.float32)
        xT_win[:, pad:] = x[b, 0:q_end].T
        xo = np.ascontiguousarray(x[b, qq * QOWN:q_end])
        ka = ka_base.copy()
        if pad > 0:
            kmask = np.arange(WIN).reshape(8, 1024) < pad
            for c in range(8):
                rows = kmask[c]
                if rows.any():
                    for hl in range(HPP):
                        blk = slice(hl * 1024, hl * 1024 + 1024)
                        a = ka[:, c, 0, blk]
                        a[:, rows] = NEG
                        ka[:, c, 0, blk] = a
                        ka[:, c, 1, blk][:, rows] = 0.0
                        ka[:, c, 2, blk][:, rows] = 0.0
        in_maps.append({
            "xT": xT_win, "xo": xo, "kac": ka, "qaug": qaug, "masks": masks,
            "identc": ident, "wkvp": wkvp, "wqp": wqp, "wop": wop,
            "wgp": w_g, "wup": w_u, "wdp": w_d,
        })

    res = run_bass_kernel_spmd(nc, in_maps, core_ids=list(range(NCORES)))
    out = np.empty((B, L, E), np.float32)
    for core in range(NCORES):
        b, qq = core // 4, core % 4
        out[b, qq * QOWN:(qq + 1) * QOWN] = res.results[core]["y"]
    return out


if __name__ == "__main__":
    build_nc()
    print("build ok")


# version 69 (final)
# speedup vs baseline: 1.1767x; 1.1767x over previous
import contextlib
import numpy as np
import ml_dtypes
import concourse.bass as bass
import concourse.mybir as mybir
from concourse.bass_utils import run_bass_kernel_spmd

F32 = mybir.dt.float32
BF16 = mybir.dt.bfloat16
I32 = mybir.dt.int32
I8 = mybir.dt.int8
AF = mybir.ActivationFunctionType
ALU = mybir.AluOpType

B, L, E, H, D = 2, 8192, 1024, 16, 64
HID = 3072
NCORES = 8
WIN = 8192
QOWN = 2048
Q0 = WIN - QOWN
LN_EPS = 1e-5
DEQ = 1.0 / 127.0
NEG = -1.0e30
BAND = 130.0
NPASS, HPP = 4, 4
SLOPES = [2.0 ** (-(h + 1) / 2.0) for h in range(H)]

ENGINES = ("sync", "gpsimd", "tensor", "scalar", "vector")


class Sem:
    def __init__(self, name):
        self.name = name
        self.handle = None
        self.n = 0


class Prog:
    def __init__(self):
        self.items = {e: [] for e in ENGINES}

    def op(self, eng, fn, sem=None, inc=1):
        if sem is None:
            self.items[eng].append(("op", fn, None, 0))
        else:
            self.items[eng].append(("op", fn, sem, inc))
            sem.n += inc

    def wait(self, eng, sem, val=None):
        v = sem.n if val is None else val
        if v > 0:
            self.items[eng].append(("wait", None, sem, v))

    def verify(self):
        ptr = {e: 0 for e in ENGINES}
        vals = {}
        progress = True
        while progress:
            progress = False
            for e in ENGINES:
                while ptr[e] < len(self.items[e]):
                    kind, fn, sem, v = self.items[e][ptr[e]]
                    if kind == "wait" and vals.get(id(sem), 0) < v:
                        break
                    if kind == "op" and sem is not None:
                        vals[id(sem)] = vals.get(id(sem), 0) + v
                    ptr[e] += 1
                    progress = True
        stuck = {e: ptr[e] for e in ENGINES if ptr[e] < len(self.items[e])}
        if stuck:
            lines = []
            for e, pt in stuck.items():
                kind, fn, sem, v = self.items[e][pt]
                lines.append(f"  {e}@{pt}/{len(self.items[e])}: wait "
                             f"{sem.name} >= {v} (have {vals.get(id(sem), 0)})")
            raise RuntimeError("DEADLOCK:\n" + "\n".join(lines))

    def emit(self, block):
        def mk(eng_name):
            def inner(eng):
                for kind, fn, sem, v in self.items[eng_name]:
                    if kind == "wait":
                        eng.wait_ge(sem.handle, v)
                    else:
                        inst = fn(eng)
                        if sem is not None:
                            inst.then_inc(sem.handle, v)
            return inner
        for e in ENGINES:
            getattr(block, e)(mk(e))


class Slots:
    def __init__(self, n):
        self.rec = [None] * n

    def wait_free(self, p, eng, i):
        r = self.rec[i]
        if r is not None:
            for sem, v in r:
                p.wait(eng, sem, v)

    def set_free(self, i, *sems):
        self.rec[i] = [(sem, sem.n) for sem in sems]


def band_lo_kt(h_global, q0):
    band = BAND / SLOPES[h_global]
    return max(0, int(np.floor((q0 - band) / 128.0)))


def build_nc():
    nc = bass.Bass("TRN2", target_bir_lowering=False, debug=False)

    xT_d = nc.dram_tensor("xT", [E, WIN], F32, kind="ExternalInput").ap()
    xo_d = nc.dram_tensor("xo", [QOWN, E], F32, kind="ExternalInput").ap()
    ka_d = nc.dram_tensor("kac", [NPASS, 8, 3, HPP * 1024], BF16, kind="ExternalInput").ap()
    qaug_d = nc.dram_tensor("qaug", [H, 4, QOWN], BF16, kind="ExternalInput").ap()
    masks_d = nc.dram_tensor("masks", [4, 128, 512], F32, kind="ExternalInput").ap()
    id_d = nc.dram_tensor("identc", [128, 128], BF16, kind="ExternalInput").ap()
    wkv_d = nc.dram_tensor("wkvp", [NPASS, E, 512], F32, kind="ExternalInput").ap()
    wq_d = nc.dram_tensor("wqp", [NPASS, E, 256], F32, kind="ExternalInput").ap()
    wo_d = nc.dram_tensor("wop", [E, E], F32, kind="ExternalInput").ap()
    wg_d = nc.dram_tensor("wgp", [24, 128, 1024], I8, kind="ExternalInput").ap()
    wu_d = nc.dram_tensor("wup", [24, 128, 1024], I8, kind="ExternalInput").ap()
    wd_d = nc.dram_tensor("wdp", [12, 128, 2048], I8, kind="ExternalInput").ap()
    y_d = nc.dram_tensor("y", [QOWN, E], F32, kind="ExternalOutput").ap()

    xn_img = nc.dram_tensor("xn_img", [E, WIN], BF16).ap()
    ctxT_img = nc.dram_tensor("ctxT_img", [E, QOWN], BF16).ap()
    s_img = nc.dram_tensor("s_img", [QOWN, E], F32).ap()

    with contextlib.ExitStack() as stack:
        ent = stack.enter_context

        ab = ent(nc.sbuf_tensor("ab", [128, 57216], BF16))
        af = ent(nc.sbuf_tensor("af", [128, 20608], F32))

        class Arena:
            def __init__(self, t):
                self.t, self.off, self.marks = t, 0, []

            def alloc(self, cols, part=128):
                a = self.off
                self.off += cols
                assert self.off <= self.t.shape[1], (self.t.shape[1], self.off)
                return self.t[0:part, a:a + cols]

            def mark(self):
                self.marks.append(self.off)

            def release(self):
                self.off = self.marks.pop()

        AB, AFA = Arena(ab), Arena(af)

        ident = AB.alloc(128)
        ones_col = AB.alloc(1)
        ones_row = AB.alloc(128, part=1)
        masks = [AFA.alloc(512) for _ in range(4)]
        ctxa = AFA.alloc(4160)
        rec = AFA.alloc(16)
        ctxn = [AB.alloc(256) for _ in range(2)]
        ctt = [AB.alloc(128) for _ in range(8)]

        AB.mark(); AFA.mark()
        xt = [AFA.alloc(512) for _ in range(8)]
        xtq = AFA.alloc(2048)
        ones_rf = AFA.alloc(128, part=1)
        mbc_f = AFA.alloc(512)
        rbc_f = AFA.alloc(512)
        t1f = AFA.alloc(512)
        t1g = AFA.alloc(512)
        strow = AFA.alloc(2048, part=1)
        srow2 = AFA.alloc(512, part=1)
        xb16 = [AB.alloc(4096) for _ in range(2)]
        sq16 = AB.alloc(4096)
        xn16 = [AB.alloc(4096) for _ in range(2)]
        wkv_sb = AB.alloc(4096)
        wq_sb = AB.alloc(2048)
        K_aug = [AB.alloc(4096, part=96) for _ in range(2)]
        V_aug = [AB.alloc(2080) for _ in range(2)]
        Q_aug = AB.alloc(8192, part=96)
        kqst = [AB.alloc(512) for _ in range(4)]
        PT = [AB.alloc(1024) for _ in range(6)]

        def ctx_acc(hl, su):
            base = hl * 16 + su
            return ctxa[:, base * 65:base * 65 + 65]

        ps_s = [ent(nc.psum_tensor(f"ps_s{i}", [128, 1024], F32)) for i in range(2)]
        ps_c = [ent(nc.psum_tensor(f"ps_c{i}", [128, 512], F32)) for i in range(2)]
        ps_p = ent(nc.psum_tensor("ps_p", [128, 512], F32))
        ps_l = ent(nc.psum_tensor("ps_l", [128, 512], F32))
        ps_lt = ps_l[0:2, :]
        ps_tp = ps_l.bitcast(BF16)[:, 0:128]

        sem_names = [
            "ini", "xld0", "xld1", "xld2", "xld3", "xld4", "xld5", "xld6", "xld7", "cast", "sq", "stat", "row", "rstd", "bcm", "bce",
            "app", "xnl", "xns", "prj", "pev", "krs0", "krs1", "qrs6", "qrs7", "kb0", "kb1", "wld", "qau",
            "sc", "msk", "exp", "ctx", "cev", "recs", "nrm", "tp", "tpe", "cts",
            "qld", "ctl", "ctw0", "ctw1", "wol", "smm", "sev", "sst", "sld0", "sld1", "fxl", "wgl0", "wgl1", "wgl2", "wul0", "wul1", "wul2", "wdl0", "wdl1", "wdl2", "dqw", "gmm", "sgl",
            "hml", "dmm", "gsl", "xol0", "xol1", "xol2", "xol3", "sld2", "sld3", "cmb", "yst",
        ]
        S = {n: Sem(n) for n in sem_names}
        p = Prog()

        p.op("gpsimd", lambda e: e.dma_start(out=ident[:], in_=id_d),
             S["wld"], 16)
        for j in range(4):
            p.op("gpsimd", lambda e, j=j: e.dma_start(
                out=masks[j][:], in_=masks_d[j]), S["wld"], 16)
        for bi in range(2):
            p.op("vector", lambda e, bi=bi: e.memset(K_aug[bi][64:96, :], 0.0))
            p.op("vector", lambda e, bi=bi: e.memset(K_aug[bi][64:65, :], 1.0))
            p.op("vector", lambda e, bi=bi: e.memset(
                V_aug[bi][:].rearrange("p (k h c) -> p k h c", h=HPP, c=65)[:, :, :, 64:65],
                1.0), S["ini"])
        p.op("vector", lambda e: e.memset(Q_aug[64:96, :], 0.0), S["ini"])
        p.op("vector", lambda e: e.memset(ones_col[:], 1.0), S["ini"])
        p.op("vector", lambda e: e.memset(ones_row[:], 1.0), S["ini"])
        p.op("vector", lambda e: e.memset(ones_rf[:], 1.0), S["ini"])

        sl_KV0 = Slots(2)
        sl_KV0.set_free(0, S["ini"])
        sl_KV0.set_free(1, S["ini"])
        sl_xt = Slots(8)
        sl_xb = Slots(2)
        sl_xn = Slots(2)
        sl_kq = Slots(4)
        sl_KV = Slots(2)
        sl_PT = Slots(6)
        sl_ss = Slots(2)
        sl_cs = Slots(2)
        sl_pj = Slots(2)
        sl_ctxn = Slots(2)
        sl_ctt = Slots(8)

        tile_i = [0]
        xn_done = set()

        def pass_lo_chunk(pp):
            return max(0, band_lo_kt(pp * HPP + HPP - 1, Q0) // 8)

        def emit_ln_fillers(t, c, half, hidx, xns):
            xbs = xb16[hidx % 2]
            sls = [(tile_i[0] + eg) % 8 for eg in range(8)]
            tile_i[0] += 8
            app_cell = [None]

            def L0():
                for eg in range(8):
                    sl = sls[eg]
                    sl_xt.wait_free(p, "sync", sl)
                    p.op("sync", lambda e, eg=eg, c=c, half=half, sl=sl:
                         e.dma_start(
                             out=xt[sl][:],
                             in_=xT_d[eg * 128:(eg + 1) * 128,
                                      c * 1024 + half * 512:
                                      c * 1024 + half * 512 + 512]),
                         S[f"xld{sl}"], 16)

            def L1():
                sl_xb.wait_free(p, "vector", hidx % 2)
                for eg in range(4):
                    sl = sls[eg]
                    p.wait("vector", S[f"xld{sl}"])
                    p.op("vector", lambda e, eg=eg, xbs=xbs, sl=sl:
                         e.tensor_copy(xbs[:, eg * 512:(eg + 1) * 512],
                                       xt[sl][:]), S["cast"])

            def L2():
                for eg in range(4, 8):
                    sl = sls[eg]
                    p.wait("vector", S[f"xld{sl}"])
                    p.op("vector", lambda e, eg=eg, xbs=xbs, sl=sl:
                         e.tensor_copy(xbs[:, eg * 512:(eg + 1) * 512],
                                       xt[sl][:]), S["cast"])
                p.op("vector", lambda e, xbs=xbs: e.tensor_tensor(
                    out=sq16[:], in0=xbs[:], in1=xbs[:], op=ALU.mult), S["sq"])

            def L3():
                sl_pj.wait_free(p, "tensor", 0)
                sl_pj.wait_free(p, "tensor", 1)
                p.wait("tensor", S["cast"])
                p.wait("tensor", S["ini"])
                for eg in range(8):
                    p.op("tensor", lambda e, eg=eg, xbs=xbs: e.matmul(
                        ps_p[0:1, :], ones_col[:],
                        xbs[:, eg * 512:(eg + 1) * 512],
                        start=(eg == 0), stop=(eg == 7)),
                        S["stat"] if eg == 7 else None)
                p.wait("tensor", S["sq"])
                for eg in range(8):
                    p.op("tensor", lambda e, eg=eg: e.matmul(
                        ps_l[0:1, :], ones_col[:],
                        sq16[:, eg * 512:(eg + 1) * 512],
                        start=(eg == 0), stop=(eg == 7)),
                        S["stat"] if eg == 7 else None)
                p.wait("vector", S["stat"])
                p.op("vector", lambda e: e.tensor_scalar_mul(
                    out=strow[0:1, 0:512], in0=ps_p[0:1, :], scalar1=1.0 / E))
                p.op("vector", lambda e: e.tensor_scalar_mul(
                    out=strow[0:1, 512:1024], in0=ps_l[0:1, :],
                    scalar1=1.0 / E), S["row"])
                p.op("vector", lambda e: e.tensor_tensor(
                    out=strow[0:1, 1024:1536], in0=strow[0:1, 0:512],
                    in1=strow[0:1, 0:512], op=ALU.mult))
                p.op("vector", lambda e: e.tensor_tensor(
                    out=strow[0:1, 1536:2048], in0=strow[0:1, 512:1024],
                    in1=strow[0:1, 1024:1536], op=ALU.subtract))
                p.op("vector", lambda e: e.tensor_scalar_add(
                    out=strow[0:1, 1536:2048], in0=strow[0:1, 1536:2048],
                    scalar1=LN_EPS), S["row"])

            def L4():
                p.wait("scalar", S["row"])
                p.op("scalar", lambda e: e.activation(
                    out=srow2[0:1, :], in_=strow[0:1, 1536:2048], func=AF.Ln))
                p.op("scalar", lambda e: e.activation(
                    out=srow2[0:1, :], in_=srow2[0:1, :], func=AF.Exp,
                    scale=-0.5), S["rstd"])
                p.wait("tensor", S["row"])
                p.op("tensor", lambda e: e.matmul(
                    ps_p[:], ones_rf[:], strow[0:1, 0:512], start=True,
                    stop=True), S["bcm"])
                p.wait("vector", S["bcm"])
                p.op("vector", lambda e: e.tensor_copy(mbc_f[:], ps_p[:]),
                     S["bce"])
                sl_pj.set_free(0, S["bce"])
                p.wait("tensor", S["rstd"])
                p.op("tensor", lambda e: e.matmul(
                    ps_l[:], ones_rf[:], srow2[0:1, :], start=True,
                    stop=True), S["bcm"])
                p.wait("vector", S["bcm"])
                p.op("vector", lambda e: e.tensor_copy(rbc_f[:], ps_l[:]),
                     S["bce"])
                sl_pj.set_free(1, S["bce"])

            def L5():
                p.wait("gpsimd", S["bce"])
                sl_xn.wait_free(p, "vector", hidx % 2)
                for eg in range(8):
                    sl = sls[eg]
                    p.op("vector", lambda e, sl=sl: e.tensor_tensor(
                        out=t1f[:], in0=xt[sl][:], in1=mbc_f[:],
                        op=ALU.subtract))
                    p.op("vector", lambda e, eg=eg, xns=xns: e.tensor_tensor(
                        out=xns[:, eg * 512:(eg + 1) * 512], in0=t1f[:],
                        in1=rbc_f[:], op=ALU.mult), S["app"])
                    sl_xt.set_free(sl, S["app"])
                sl_xb.set_free(hidx % 2, S["app"])
                app_cell[0] = S["app"].n

            def L6():
                p.wait("gpsimd", S["app"], app_cell[0])
                for eg in range(8):
                    p.op("gpsimd", lambda e, eg=eg, c=c, half=half, xns=xns:
                         e.dma_start(
                             out=xn_img[eg * 128:(eg + 1) * 128,
                                        c * 1024 + half * 512:
                                        c * 1024 + half * 512 + 512],
                             in_=xns[:, eg * 512:(eg + 1) * 512]),
                         S["xns"], 16)

            return [L0, L1, L2, L3, L4, L5, L6], app_cell

        def emit_xn_load(c, half, hidx, xns):
            p.wait("sync", S["xns"])
            for eg in range(8):
                p.op("sync", lambda e, eg=eg, c=c, half=half, xns=xns: e.dma_start(
                    out=xns[:, eg * 512:(eg + 1) * 512],
                    in_=xn_img[eg * 128:(eg + 1) * 128,
                               c * 1024 + half * 512: c * 1024 + half * 512 + 512]),
                    S["xnl"], 16)

        def emit_ln_proj(pp, t, c, att, zip_q=True):
            own = c >= 6
            q_fillers = []
            do_ln = c not in xn_done
            if do_ln:
                xn_done.add(c)
            for half in range(2):
                hidx = 2 * t + half
                xns = xn16[hidx % 2]
                ln_fills, app_cell = [], None
                if do_ln:
                    ln_fills, app_cell = emit_ln_fillers(t, c, half, hidx, xns)
                elif half == 0:
                    for h2 in range(2):
                        sl_xn.wait_free(p, "sync", (hidx + h2) % 2)
                        emit_xn_load(c, h2, hidx + h2, xn16[(hidx + h2) % 2])
                kv_buf = c % 2
                if half == 0:
                    sl_KV.wait_free(p, "tensor", kv_buf)
                xnl_n = S["xnl"].n if not do_ln else None
                grp = 0

                def dep_wait(app_cell=app_cell):
                    if app_cell is not None:
                        p.wait("tensor", S["app"], app_cell[0])
                    if xnl_n is not None:
                        p.wait("tensor", S["xnl"], xnl_n)

                fillers = list(ln_fills)

                def mk_kproj(mg, xns=xns, kv_buf=kv_buf, half=half,
                             wg=None):
                    wg = wld_goal[0]
                    def fn_a():
                        pj = [ps_p, ps_l][mg % 2]
                        p.wait("tensor", S["wld"], wg)
                        dep_wait()
                        sl_pj.wait_free(p, "tensor", mg % 2)
                        sl_kq.wait_free(p, "tensor", mg)
                        for eg in range(4):
                            p.op("tensor", lambda e, eg=eg, mg=mg, xns=xns, pj=pj:
                                 e.matmul(
                                     pj[:],
                                     wkv_sb[:, eg * 512 + mg * 128: eg * 512 + mg * 128 + 128],
                                     xns[:, eg * 512:(eg + 1) * 512],
                                     start=(eg == 0), stop=False), None)
                    def fn():
                        stg = kqst[mg]
                        pj = [ps_p, ps_l][mg % 2]
                        for eg in range(4, 8):
                            p.op("tensor", lambda e, eg=eg, mg=mg, xns=xns, pj=pj:
                                 e.matmul(
                                     pj[:],
                                     wkv_sb[:, eg * 512 + mg * 128: eg * 512 + mg * 128 + 128],
                                     xns[:, eg * 512:(eg + 1) * 512],
                                     start=False, stop=(eg == 7)),
                                 S["prj"] if eg == 7 else None)
                        p.wait("vector", S["prj"])
                        p.op("vector", lambda e, stg=stg, pj=pj: e.tensor_copy(
                            stg[:], pj[:]), S["pev"])
                        sl_pj.set_free(mg % 2, S["pev"])
                        p.wait("gpsimd", S["pev"])
                        skr = S[f"krs{kv_buf}"]
                        for hh in range(2):
                            hloc = 2 * mg + hh
                            p.op("gpsimd", lambda e, stg=stg, hh=hh, hloc=hloc,
                                 half=half, kv_buf=kv_buf: e.dma_start(
                                     out=K_aug[kv_buf][0:64,
                                                       hloc * 1024 + half * 512:
                                                       hloc * 1024 + half * 512 + 512],
                                     in_=stg[64 * hh:64 * hh + 64, :]), skr, 16)
                        sl_kq.set_free(mg, skr)
                    return [fn_a, fn]

                def mk_vproj(tt, xns=xns, kv_buf=kv_buf, half=half):
                    wg = wld_goal[0]
                    def fn_a():
                        pj = [ps_p, ps_l][tt % 2]
                        p.wait("tensor", S["wld"], wg)
                        dep_wait()
                        sl_pj.wait_free(p, "tensor", tt % 2)
                        for eg in range(4):
                            p.op("tensor", lambda e, tt=tt, eg=eg, xns=xns, pj=pj:
                                 e.matmul(
                                     pj[:, 0:256],
                                     xns[:, eg * 512 + tt * 128: eg * 512 + tt * 128 + 128],
                                     wkv_sb[:, eg * 512 + 256: eg * 512 + 512],
                                     start=(eg == 0), stop=False), None)
                    def fn():
                        ktl = 4 * half + tt
                        pj = [ps_p, ps_l][tt % 2]
                        for eg in range(4, 8):
                            p.op("tensor", lambda e, tt=tt, eg=eg, xns=xns, pj=pj:
                                 e.matmul(
                                     pj[:, 0:256],
                                     xns[:, eg * 512 + tt * 128: eg * 512 + tt * 128 + 128],
                                     wkv_sb[:, eg * 512 + 256: eg * 512 + 512],
                                     start=False, stop=(eg == 7)),
                                 S["prj"] if eg == 7 else None)
                        p.wait("vector", S["prj"])
                        p.op("vector", lambda e, ktl=ktl, kv_buf=kv_buf, pj=pj:
                             e.tensor_copy(
                                 V_aug[kv_buf][:, ktl * 260: ktl * 260 + 260]
                                 .rearrange("p (h c) -> p h c", c=65)[:, :, 0:64],
                                 pj[:, 0:256].rearrange("p (h c) -> p h c", c=64)),
                             S["pev"])
                        sl_pj.set_free(tt % 2, S["pev"])
                    return [fn_a, fn]

                def mk_qproj(mg, xns=xns, c=c, half=half):
                    wg = wld_goal[0]
                    def fn_a():
                        pj = [ps_p, ps_l][mg % 2]
                        p.wait("tensor", S["wld"], wg)
                        dep_wait()
                        sl_pj.wait_free(p, "tensor", mg % 2)
                        sl_kq.wait_free(p, "tensor", 2 + mg)
                        for eg in range(4):
                            p.op("tensor", lambda e, mg=mg, eg=eg, xns=xns, pj=pj:
                                 e.matmul(
                                     pj[:],
                                     wq_sb[:, eg * 256 + mg * 128: eg * 256 + mg * 128 + 128],
                                     xns[:, eg * 512:(eg + 1) * 512],
                                     start=(eg == 0), stop=False), None)
                    def fn():
                        stg = kqst[2 + mg]
                        pj = [ps_p, ps_l][mg % 2]
                        for eg in range(4, 8):
                            p.op("tensor", lambda e, mg=mg, eg=eg, xns=xns, pj=pj:
                                 e.matmul(
                                     pj[:],
                                     wq_sb[:, eg * 256 + mg * 128: eg * 256 + mg * 128 + 128],
                                     xns[:, eg * 512:(eg + 1) * 512],
                                     start=False, stop=(eg == 7)),
                                 S["prj"] if eg == 7 else None)
                        p.wait("vector", S["prj"])
                        p.op("vector", lambda e, stg=stg, pj=pj: e.tensor_scalar_mul(
                            out=stg[:], in0=pj[:], scalar1=0.125), S["pev"])
                        sl_pj.set_free(mg % 2, S["pev"])
                        p.wait("gpsimd", S["pev"])
                        qcol = (c - 6) * 1024 + half * 512
                        sqr = S[f"qrs{c}"]
                        for hh in range(2):
                            hloc = 2 * mg + hh
                            p.op("gpsimd", lambda e, stg=stg, hh=hh, hloc=hloc,
                                 qcol=qcol: e.dma_start(
                                     out=Q_aug[0:64, hloc * 2048 + qcol:
                                               hloc * 2048 + qcol + 512],
                                     in_=stg[64 * hh:64 * hh + 64, :]), sqr, 16)
                        sl_kq.set_free(2 + mg, sqr)
                    return [fn_a, fn]

                for mg in range(2):
                    fillers.extend(mk_kproj(mg))
                for tt in range(4):
                    fillers.extend(mk_vproj(tt))
                if own:
                    for mg in range(2):
                        (fillers if zip_q else q_fillers).extend(mk_qproj(mg))

                def post_fn(hidx=hidx, do_ln=do_ln):
                    if do_ln:
                        sl_xn.set_free(hidx % 2, S["prj"], S["xns"])
                    else:
                        sl_xn.set_free(hidx % 2, S["prj"])
                (fillers if zip_q else q_fillers).append(post_fn)

                if att is not None:
                    emit_attention(att[0], att[1], att[2],
                                   [2 * half, 2 * half + 1], fillers)
                for fn in fillers:
                    fn()
                fillers.clear()
                for fn in q_fillers:
                    qproj_pending.append(fn)
                q_fillers = []
            kv_buf = c % 2
            sl_KV0.wait_free(p, "sync", kv_buf)
            sl_KV.wait_free(p, "sync", kv_buf)
            p.op("sync", lambda e, pp=pp, c=c, kv_buf=kv_buf: e.dma_start(
                out=K_aug[kv_buf][65:68, :], in_=ka_d[pp, c]), S[f"kb{kv_buf}"], 16)
            return {"krs": S[f"krs{kv_buf}"].n, "pev": S["pev"].n,
                    "kb": S[f"kb{kv_buf}"].n, "qrs6": S["qrs6"].n,
                    "qrs7": S["qrs7"].n}

        pair_i = [0]
        pending = []

        def emit_attention(pp, c, snap, heads, fillers=None):
            kv_buf = c % 2
            fill_ctr = [0]
            p.wait("tensor", S[f"krs{kv_buf}"], snap["krs"])
            p.wait("tensor", S[f"kb{kv_buf}"], snap["kb"])
            p.wait("tensor", S["pev"], snap["pev"])
            p.wait("tensor", S["qrs6"], snap["qrs6"])
            p.wait("tensor", S["qrs7"], snap["qrs7"])
            p.wait("tensor", S["qau"])
            p.wait("tensor", S["ini"])
            p.wait("vector", S["wld"], 80)
            for hl in heads:
                hg = pp * HPP + hl
                for m in range(4):
                    q0 = Q0 + 512 * m
                    hi_kt = min(8 * c + 7, (q0 + 511) // 128)
                    lo_kt = max(8 * c, band_lo_kt(hg, q0))
                    if lo_kt > hi_kt:
                        continue
                    kts = list(range(lo_kt, hi_kt + 1))
                    cps = ps_c[m % 2]
                    def su_live(su, kt, q0=q0, hg=hg):
                        squ0 = q0 + su * 128
                        if kt * 128 > squ0 + 127:
                            return False
                        if SLOPES[hg] * (squ0 - (kt * 128 + 127)) >= BAND:
                            return False
                        return True
                    live_mms = [(kt, su) for kt in kts for su in range(4)
                                if su_live(su, kt)]
                    if not live_mms:
                        continue
                    n_mm = len(live_mms)
                    su_set = sorted({su for _, su in live_mms})
                    mm_i = 0
                    pairs = [kts[i:i + 2] for i in range(0, len(kts), 2)]
                    npairs = len(pairs)
                    for pidx, pr in enumerate(pairs):
                        fill_ctr[0] += 1
                        if fillers and fill_ctr[0] >= 2:
                            fillers.pop(0)()
                        sb = pair_i[0] % 2
                        ss = ps_s[sb]
                        sl_ss.wait_free(p, "tensor", sb)
                        offs = []
                        wrs = []
                        for j, kt in enumerate(pr):
                            ktl = kt - 8 * c
                            dj = kt - q0 // 128
                            off = 128 * dj if dj > 0 else 0
                            wr = 128 * (1 + max(su for su in range(4)
                                                if su_live(su, kt)))
                            offs.append(off)
                            wrs.append(wr)
                            p.op("tensor", lambda e, ss=ss, j=j, hl=hl, ktl=ktl,
                                 m=m, kv_buf=kv_buf, off=off, wr=wr: e.matmul(
                                     ss[:, 512 * j + off:512 * j + wr],
                                     K_aug[kv_buf][:, hl * 1024 + ktl * 128:
                                                   hl * 1024 + ktl * 128 + 128],
                                     Q_aug[:, hl * 2048 + m * 512 + off:
                                           hl * 2048 + m * 512 + wr],
                                     start=True, stop=True), S["sc"])
                        nmask = 0
                        for j, kt in enumerate(pr):
                            dj = kt - q0 // 128
                            if dj >= 0:
                                p.wait("vector", S["sc"])
                                p.op("vector", lambda e, ss=ss, j=j, dj=dj:
                                     e.tensor_tensor(
                                         out=ss[:, 512 * j + 128 * dj:
                                                512 * j + 128 * dj + 128],
                                         in0=ss[:, 512 * j + 128 * dj:
                                                512 * j + 128 * dj + 128],
                                         in1=masks[dj][:, 128 * dj:128 * dj + 128],
                                         op=ALU.add), S["msk"])
                                nmask += 1
                        pts = PT[pair_i[0] % 6]
                        sl_PT.wait_free(p, "scalar", pair_i[0] % 6)
                        p.wait("scalar", S["sc"])
                        if nmask:
                            p.wait("scalar", S["msk"])
                        w = 512 * (len(pr) - 1) + wrs[-1]
                        o0 = offs[0]
                        p.op("scalar", lambda e, pts=pts, ss=ss, w=w, o0=o0:
                             e.activation(out=pts[:, o0:w], in_=ss[:, o0:w],
                                          func=AF.Exp), S["exp"])
                        sl_ss.set_free(sb, S["exp"])

                        mm_specs = []
                        for j, kt in enumerate(pr):
                            for su in range(4):
                                if not su_live(su, kt):
                                    continue
                                mm_specs.append((j, kt, su, mm_i == 0,
                                                 mm_i == n_mm - 1))
                                mm_i += 1
                        is_last_pair = (pidx == npairs - 1)
                        is_first_pair = (pidx == 0)
                        my_slot = pair_i[0] % 6
                        exp_n = S["exp"].n

                        def ctx_fn(mm_specs=mm_specs, cps=cps, kv_buf=kv_buf,
                                   hl=hl, m=m, c=c, pts=pts, my_slot=my_slot,
                                   exp_n=exp_n, is_last_pair=is_last_pair,
                                   is_first_pair=is_first_pair,
                                   su_set=su_set, q0=q0):
                            if is_first_pair:
                                sl_cs.wait_free(p, "tensor", m % 2)
                            p.wait("tensor", S["exp"], exp_n)
                            for j, kt, su, first, last in mm_specs:
                                p.op("tensor", lambda e, cps=cps, su=su, j=j,
                                     kt=kt, c=c, kv_buf=kv_buf, hl=hl, pts=pts,
                                     first=first, last=last: e.matmul(
                                         cps[:, su * 65: su * 65 + 65],
                                         pts[:, 512 * j + su * 128:
                                             512 * j + su * 128 + 128],
                                         V_aug[kv_buf][:, (kt - 8 * c) * 260 + hl * 65:
                                                       (kt - 8 * c) * 260 + hl * 65 + 65],
                                         start=first, stop=last), S["ctx"])
                            sl_PT.set_free(my_slot, S["ctx"])
                            if is_last_pair:
                                p.op("tensor", lambda e: e.nop(), S["ctx"])
                                p.wait("vector", S["ctx"])
                                first_ever = (c == (q0 + 511) // 1024)
                                for su in range(su_set[0], su_set[-1] + 1):
                                    acc = ctx_acc(hl, 4 * m + su)
                                    if first_ever:
                                        p.op("vector", lambda e, acc=acc, cps=cps,
                                             su=su: e.tensor_copy(
                                                 acc, cps[:, su * 65:su * 65 + 65]),
                                             S["cev"])
                                    else:
                                        p.op("vector", lambda e, acc=acc, cps=cps,
                                             su=su: e.tensor_tensor(
                                                 out=acc,
                                                 in0=cps[:, su * 65:su * 65 + 65],
                                                 in1=acc, op=ALU.add), S["cev"])
                                sl_cs.set_free(m % 2, S["cev"])

                        pending.append(ctx_fn)
                        if len(pending) > 5:
                            pending.pop(0)()
                        pair_i[0] += 1
            if heads[-1] == 3:
                while pending:
                    pending.pop(0)()
                p.op("tensor", lambda e: e.nop(), S["sc"])
                sl_KV.set_free(kv_buf, S["sc"])

        def emit_finalize(pp, zipper=None):
            p.wait("vector", S["ctx"])
            p.wait("scalar", S["cev"])
            nrm_when = {}
            prev_tp = [None]
            zi = [0]
            for su in range(16):
                if zipper:
                    while zi[0] < (su * 3) // 2 and zipper:
                        zipper.pop(0)()
                        zi[0] += 1
                if su >= 4:
                    p.wait("vector", S["nrm"], nrm_when[su - 4])
                for hl in range(HPP):
                    acc = ctx_acc(hl, su)
                    sl = (su * HPP + hl) % 16
                    p.op("vector", lambda e, acc=acc, sl=sl: e.reciprocal(
                        rec[:, sl:sl + 1], acc[:, 64:65]), S["recs"])
                cn = ctxn[su % 2]
                sl_ctxn.wait_free(p, "scalar", su % 2)
                p.wait("scalar", S["recs"])
                for hl in range(HPP):
                    acc = ctx_acc(hl, su)
                    sl = (su * HPP + hl) % 16
                    if hl < 2:
                        p.op("scalar", lambda e, acc=acc, cn=cn, hl=hl, sl=sl:
                             e.activation(out=cn[:, hl * 64:(hl + 1) * 64],
                                          in_=acc[:, 0:64], func=AF.Copy,
                                          scale=rec[:, sl:sl + 1]), S["nrm"])
                    else:
                        p.op("vector", lambda e, acc=acc, cn=cn, hl=hl, sl=sl:
                             e.tensor_scalar_mul(
                                 out=cn[:, hl * 64:(hl + 1) * 64],
                                 in0=acc[:, 0:64],
                                 scalar1=rec[:, sl:sl + 1]), S["nrm"])
                nrm_when[su] = S["nrm"].n

                def tp_fn(su=su, cn=cn, pp=pp, nrm_n=S["nrm"].n):
                    p.wait("tensor", S["nrm"], nrm_n)
                    for hp in range(2):
                        cslot = (su * 2 + hp) % 8
                        ct = ctt[cslot]
                        tb, trk, tix = [(ps_p, sl_pj, 0), (ps_l, sl_pj, 1),
                                        (ps_c[0], sl_cs, 0),
                                        (ps_c[1], sl_cs, 1)][cslot % 4]
                        tbv = tb.bitcast(BF16)[:, 0:128]
                        sl_ctt.wait_free(p, "tensor", cslot)
                        trk.wait_free(p, "tensor", tix)
                        p.op("tensor", lambda e, cn=cn, hp=hp, tbv=tbv:
                             e.transpose(tbv, cn[:, hp * 128:(hp + 1) * 128],
                                         ident[:]), S["tp"])
                        p.wait("vector", S["tp"])
                        p.op("vector", lambda e, ct=ct, tbv=tbv: e.tensor_copy(
                            ct[:], tbv), S["tpe"])
                        trk.set_free(tix, S["tpe"])
                        p.wait("gpsimd", S["tpe"])
                        p.op("gpsimd", lambda e, ct=ct, pp=pp, hp=hp, su=su:
                             e.dma_start(
                                 out=ctxT_img[pp * 256 + hp * 128:
                                              pp * 256 + (hp + 1) * 128,
                                              su * 128:(su + 1) * 128],
                                 in_=ct[:]), S["cts"], 16)
                        sl_ctt.set_free(cslot, S["cts"])
                    sl_ctxn.set_free(su % 2, S["tp"])

                if prev_tp[0] is not None:
                    prev_tp[0]()
                prev_tp[0] = tp_fn
            prev_tp[0]()


        carry = [None]
        qproj_pending = []
        def emit_wload(pp):
            p.wait("sync", S["wld"])
            p.op("sync", lambda e, pp=pp: e.dma_start(
                out=xtq[:].rearrange("p (w c) -> p w c", c=256),
                in_=wq_d[pp].rearrange("(w p) c -> p w c", p=128)),
                S["qld"], 16)
            if pp == 0:
                for wi in range(8):
                    p.op("gpsimd", lambda e, wi=wi: e.dma_start(
                        out=wkv_sb[:, wi * 512:(wi + 1) * 512],
                        in_=wkv_d[0, wi * 128:(wi + 1) * 128, :]),
                        S["wld"], 16)
                return
            for wi in range(8):
                sl_xt.wait_free(p, "sync", wi)
                p.op("sync", lambda e, pp=pp, wi=wi: e.dma_start(
                    out=xt[wi][:],
                    in_=wkv_d[pp, wi * 128:(wi + 1) * 128, :]),
                    S[f"xld{wi}"], 16)

        def emit_wcast(pp):
            p.wait("vector", S["prj"])
            if pp > 0:
                for wi in range(8):
                    p.wait("vector", S[f"xld{wi}"])
                    p.op("vector", lambda e, wi=wi: e.tensor_copy(
                        wkv_sb[:, wi * 512:(wi + 1) * 512], xt[wi][:]),
                        S["wld"])
                    sl_xt.set_free(wi, S["wld"])
            p.wait("vector", S["qld"])
            for wi in range(8):
                p.op("vector", lambda e, wi=wi: e.tensor_copy(
                    wq_sb[:, wi * 256:(wi + 1) * 256],
                    xtq[:, wi * 256:wi * 256 + 256]), S["wld"])

        wld_goal = [0]
        for pp in range(NPASS):
            if pp == 0:
                emit_wload(0)
            emit_wcast(pp)
            wld_goal[0] = S["wld"].n
            p.wait("gpsimd", S["wld"])

            snaps = {}
            lo_c = pass_lo_chunk(pp)
            for t, c in enumerate(range(7, lo_c - 1, -1)):
                att = carry[0] if t == 0 else (pp, c + 1, snaps[c + 1])
                snaps[c] = emit_ln_proj(pp, t, c, att, zip_q=(t != 0))
                if t == 0:
                    p.wait("sync", S["sc"])
                    p.wait("sync", S["ini"])
                    for hl in range(HPP):
                        p.op("sync", lambda e, hl=hl, pp=pp: e.dma_start(
                            out=Q_aug[64:68, hl * 2048:(hl + 1) * 2048],
                            in_=qaug_d[pp * HPP + hl]), S["qau"], 16)
                    p.wait("gpsimd", S["sc"])
                    while qproj_pending:
                        qproj_pending.pop(0)()
                    snaps[7]["qrs6"] = S["qrs6"].n
                    snaps[7]["qrs7"] = S["qrs7"].n
                    snaps[7]["pev"] = S["pev"].n
                    if pp > 0:
                        emit_finalize(pp - 1)
                    p.wait("vector", S["nrm"])
            carry[0] = (pp, lo_c, snaps[lo_c])
            if pp + 1 < NPASS:
                emit_wload(pp + 1)

        emit_attention(carry[0][0], carry[0][1], carry[0][2],
                       [0, 1, 2, 3], None)

        AB.release(); AFA.release()
        AB.mark(); AFA.mark()
        h_sb = AB.alloc(24 * 1024)
        xnT_sb = AB.alloc(8192)
        wg_bf = [AB.alloc(1024) for _ in range(3)]
        wu_bf = [AB.alloc(1024) for _ in range(3)]
        wd_bf = [AB.alloc(2048) for _ in range(3)]
        sg_bf = [AB.alloc(512) for _ in range(2)]
        ctxw = [AB.alloc(1024) for _ in range(2)]
        woT_sb = xnT_sb
        gst = [AFA.alloc(512) for _ in range(4)]
        xos = [AFA.alloc(512) for _ in range(4)]
        sld = [AFA.alloc(512) for _ in range(4)]
        outs = [AFA.alloc(512) for _ in range(4)]
        sst = [gst[0], gst[1], sld[0], sld[1]]
        wg_i = [AFA.alloc(256).bitcast(I8) for _ in range(2)]
        wu_i = [AFA.alloc(256).bitcast(I8) for _ in range(2)]
        wd_i = [AFA.alloc(512).bitcast(I8) for _ in range(3)]

        sl_wgi = Slots(2)
        sl_wui = Slots(2)
        sl_wdi = Slots(3)
        sl_wgb = Slots(3)
        sl_wub = Slots(3)
        sl_wdb = Slots(3)
        sl_sg = Slots(2)
        sl_gst = Slots(4)
        sl_xo = Slots(4)
        sl_out = Slots(4)
        sl_sld = Slots(4)
        sl_sst = Slots(4)
        sl_ctw = Slots(2)
        sl_dps = Slots(8)

        down_banks = [ps_s[0][:, 0:512], ps_s[0][:, 512:1024],
                      ps_s[1][:, 0:512], ps_s[1][:, 512:1024],
                      ps_c[0][:], ps_c[1][:], ps_p[:], ps_l[:]]

        def ffn_load_deq(hc):
            gslot = hc % 3
            islot = hc % 2
            sl_wgi.wait_free(p, "sync", islot)
            p.op("sync", lambda e, hc=hc, islot=islot: e.dma_start(
                out=wg_i[islot][:], in_=wg_d[hc]), S[f"wgl{islot}"], 16)
            sl_wui.wait_free(p, "sync", islot)
            p.op("sync", lambda e, hc=hc, islot=islot: e.dma_start(
                out=wu_i[islot][:], in_=wu_d[hc]), S[f"wul{islot}"], 16)
            p.wait("scalar", S[f"wgl{islot}"])
            p.wait("scalar", S[f"wul{islot}"])
            sl_wgb.wait_free(p, "scalar", gslot)
            p.op("scalar", lambda e, gslot=gslot, islot=islot: e.activation(
                out=wg_bf[gslot][:], in_=wg_i[islot][:], func=AF.Copy),
                S["dqw"])
            sl_wgi.set_free(islot, S["dqw"])
            sl_wub.wait_free(p, "scalar", gslot)
            p.op("scalar", lambda e, gslot=gslot, islot=islot: e.activation(
                out=wu_bf[gslot][:], in_=wu_i[islot][:], func=AF.Copy),
                S["dqw"])
            sl_wui.set_free(islot, S["dqw"])

        def mk_gate_unit(tg, hc):
            def fn():
                gslot = hc % 3
                if hc + 2 < 24:
                    ffn_load_deq(hc + 2)
                p.wait("tensor", S["dqw"],
                       S["dqw"].n - (2 if hc + 2 < 24 else 0))
                p.wait("tensor", S["fxl"])
                for nh in range(2):
                    gb = down_banks[nh]
                    ub = down_banks[2 + nh]
                    sl_dps.wait_free(p, "tensor", nh)
                    sl_dps.wait_free(p, "tensor", 2 + nh)
                    for eg in range(8):
                        p.op("tensor", lambda e, gb=gb, eg=eg, nh=nh, gslot=gslot:
                             e.matmul(gb, wg_bf[gslot][:, eg * 128:(eg + 1) * 128],
                                      xnT_sb[:, eg * 1024 + nh * 512:
                                             eg * 1024 + nh * 512 + 512],
                                      start=(eg == 0), stop=(eg == 7)),
                             S["gmm"] if eg == 7 else None)
                    for eg in range(8):
                        p.op("tensor", lambda e, ub=ub, eg=eg, nh=nh, gslot=gslot:
                             e.matmul(ub, wu_bf[gslot][:, eg * 128:(eg + 1) * 128],
                                      xnT_sb[:, eg * 1024 + nh * 512:
                                             eg * 1024 + nh * 512 + 512],
                                      start=(eg == 0), stop=(eg == 7)),
                             S["gmm"] if eg == 7 else None)
                    sgs = sg_bf[nh]
                    sl_sg.wait_free(p, "scalar", nh)
                    p.wait("scalar", S["gmm"], S["gmm"].n - 1)
                    p.op("scalar", lambda e, gb=gb, sgs=sgs: e.activation(
                        out=sgs[:], in_=gb, func=AF.Silu, scale=DEQ), S["sgl"])
                    sl_dps.set_free(nh, S["sgl"])
                    p.wait("vector", S["gmm"])
                    p.wait("vector", S["sgl"])
                    p.op("vector", lambda e, ub=ub, sgs=sgs, hc=hc, nh=nh:
                         e.scalar_tensor_tensor(
                             out=h_sb[:, hc * 1024 + nh * 512:
                                      hc * 1024 + nh * 512 + 512],
                             in0=ub, scalar=DEQ, in1=sgs[:],
                             op0=ALU.mult, op1=ALU.mult), S["hml"])
                    sl_dps.set_free(2 + nh, S["hml"])
                    sl_sg.set_free(nh, S["hml"])
                sl_wgb.set_free(gslot, S["gmm"])
                sl_wub.set_free(gslot, S["gmm"])
            return fn

        def emit_gate_prologue(tg, preloaded=False):
            if tg > 0:
                p.wait("sync", S["gmm"])
                p.wait("sync", S["smm"])
            p.wait("sync", S["xns"])
            for eg in range(8):
                p.op("sync", lambda e, eg=eg, tg=tg: e.dma_start(
                    out=xnT_sb[:, eg * 1024:(eg + 1) * 1024],
                    in_=xn_img[eg * 128:(eg + 1) * 128,
                               Q0 + tg * 1024: Q0 + (tg + 1) * 1024]),
                    S["fxl"], 16)
            if not preloaded:
                ffn_load_deq(0)
                ffn_load_deq(1)

        p.wait("tensor", S["exp"])
        p.wait("tensor", S["cev"])
        for sem in ("sc", "ctx", "prj", "pev", "exp", "cev",
                    "krs0", "krs1", "qrs6", "qrs7", "app", "xnl"):
            p.wait("sync", S[sem])
        emit_gate_prologue(0)
        gate_units = [mk_gate_unit(0, hc) for hc in range(24)]
        emit_finalize(NPASS - 1, gate_units)
        while gate_units:
            gate_units.pop(0)()

        p.wait("gpsimd", S["cts"])
        p.wait("gpsimd", S["wld"])
        p.wait("gpsimd", S["gmm"])
        p.op("gpsimd", lambda e: e.dma_start(
            out=woT_sb[:].rearrange("p (a c) -> p a c", a=8),
            in_=wo_d.rearrange("(a p) c -> p a c", p=128)), S["wol"], 16)
        p.wait("tensor", S["wol"])
        p.wait("tensor", S["sgl"])
        p.wait("tensor", S["hml"])
        p.wait("sync", S["cts"])

        def ctw_load(su):
            cw = ctxw[su % 2]
            scw = S[f"ctw{su % 2}"]
            sl_ctw.wait_free(p, "sync", su % 2)
            for eg in range(8):
                p.op("sync", lambda e, eg=eg, su=su, cw=cw: e.dma_start(
                    out=cw[:, eg * 128:(eg + 1) * 128],
                    in_=ctxT_img[eg * 128:(eg + 1) * 128,
                                 su * 128:(su + 1) * 128]), scw, 16)
            return S[f"ctw{su % 2}"].n

        ctw_when = {0: ctw_load(0)}
        for su in range(16):
            cw = ctxw[su % 2]
            if su + 1 < 16:
                ctw_when[su + 1] = ctw_load(su + 1)
            spb = ps_s[su % 2]
            sl_ss.wait_free(p, "tensor", su % 2)
            p.wait("tensor", S[f"ctw{su % 2}"], ctw_when[su])
            for eh in range(2):
                for eg in range(8):
                    p.op("tensor", lambda e, spb=spb, cw=cw, eh=eh, eg=eg:
                         e.matmul(
                             spb[:, eh * 512:(eh + 1) * 512],
                             cw[:, eg * 128:(eg + 1) * 128],
                             woT_sb[:, eg * 1024 + eh * 512:
                                    eg * 1024 + eh * 512 + 512],
                             start=(eg == 0), stop=(eg == 7)),
                         S["smm"] if eg == 7 else None)
            sl_ctw.set_free(su % 2, S["smm"])
            p.wait("vector", S["smm"])
            for eh in range(2):
                sslot = (su * 2 + eh) % 4
                sl_sst.wait_free(p, "vector", sslot)
                p.op("vector", lambda e, spb=spb, eh=eh, sslot=sslot:
                     e.tensor_copy(sst[sslot][:],
                                   spb[:, eh * 512:(eh + 1) * 512]), S["sev"])
                p.wait("gpsimd", S["sev"])
                p.op("gpsimd", lambda e, su=su, eh=eh, sslot=sslot: e.dma_start(
                    out=s_img[su * 128:(su + 1) * 128, eh * 512:(eh + 1) * 512],
                    in_=sst[sslot][:]), S["sst"], 16)
                sl_sst.set_free(sslot, S["sst"])
            sl_ss.set_free(su % 2, S["sev"])

        def wd_load_deq(hcp):
            dslot = hcp % 3
            sl_wdi.wait_free(p, "sync", dslot)
            p.op("sync", lambda e, hcp=hcp, dslot=dslot: e.dma_start(
                out=wd_i[dslot][:], in_=wd_d[hcp]), S[f"wdl{dslot}"], 16)
            p.wait("scalar", S[f"wdl{dslot}"])
            sl_wdb.wait_free(p, "scalar", dslot)
            p.op("scalar", lambda e, dslot=dslot: e.activation(
                out=wd_bf[dslot][:], in_=wd_i[dslot][:], func=AF.Copy),
                S["dqw"])
            sl_wdi.set_free(dslot, S["dqw"])

        def emit_down(tg, preloaded=False):
            p.wait("tensor", S["hml"])
            p.wait("tensor", S["sev"])
            p.wait("scalar", S["sst"])
            for th in range(2):
                if th == 0 and not preloaded:
                    wd_load_deq(0)
                    wd_load_deq(1)
                for hcp in range(12):
                    dslot = hcp % 3
                    nxt = hcp + 2
                    if nxt < 12:
                        wd_load_deq(nxt)
                    elif th == 0:
                        wd_load_deq(nxt - 12)
                    p.wait("tensor", S["dqw"],
                           S["dqw"].n - (2 if (nxt < 12 or th == 0) else
                                         (1 if hcp < 11 else 0)))
                    for sub in range(2):
                        hc = 2 * hcp + sub
                        for bi in range(8):
                            eh, t4 = bi // 4, bi % 4
                            tt = th * 4 + t4
                            bk = down_banks[bi]
                            if hc == 0:
                                sl_dps.wait_free(p, "tensor", bi)
                            p.op("tensor", lambda e, bk=bk, hc=hc, tt=tt, eh=eh,
                                 dslot=dslot, sub=sub: e.matmul(
                                     bk,
                                     h_sb[:, hc * 1024 + tt * 128:
                                          hc * 1024 + tt * 128 + 128],
                                     wd_bf[dslot][:, sub * 1024 + eh * 512:
                                                  sub * 1024 + eh * 512 + 512],
                                     start=(hc == 0), stop=(hc == 23)),
                                 S["dmm"] if bi == 7 else None)
                    sl_wdb.set_free(dslot, S["dmm"])
                def tail_load(j, tg=tg, th=th):
                    gslot = j % 4
                    eh, t4 = j // 4, j % 4
                    tt = th * 4 + t4
                    sl_xo.wait_free(p, "sync", gslot)
                    p.op("sync", lambda e, tg=tg, tt=tt, gslot=gslot, eh=eh:
                         e.dma_start(
                             out=xos[gslot][:],
                             in_=xo_d[tg * 1024 + tt * 128:
                                      tg * 1024 + tt * 128 + 128,
                                      eh * 512:(eh + 1) * 512]),
                         S[f"xol{gslot}"], 16)
                    p.wait("sync", S["sst"])
                    sl_sld.wait_free(p, "sync", gslot)
                    p.op("sync", lambda e, tg=tg, tt=tt, gslot=gslot, eh=eh:
                         e.dma_start(
                             out=sld[gslot][:],
                             in_=s_img[tg * 1024 + tt * 128:
                                       tg * 1024 + tt * 128 + 128,
                                       eh * 512:(eh + 1) * 512]),
                         S[f"sld{gslot}"], 16)

                for j in range(4):
                    tail_load(j)
                for bi in range(8):
                    eh, t4 = bi // 4, bi % 4
                    tt = th * 4 + t4
                    bk = down_banks[bi]
                    gslot = bi % 4
                    sl_gst.wait_free(p, "scalar", gslot)
                    p.wait("scalar", S["dmm"])
                    p.op("scalar", lambda e, bk=bk, gslot=gslot: e.activation(
                        out=gst[gslot][:], in_=bk, func=AF.Silu, scale=DEQ),
                        S["gsl"])
                    sl_dps.set_free(bi, S["gsl"])
                    sl_out.wait_free(p, "vector", gslot)
                    p.wait("vector", S["gsl"])
                    p.wait("vector", S[f"xol{gslot}"])
                    p.wait("vector", S[f"sld{gslot}"])
                    p.op("vector", lambda e, gslot=gslot: e.tensor_tensor(
                        out=outs[gslot][:], in0=gst[gslot][:],
                        in1=sld[gslot][:], op=ALU.mult))
                    p.op("vector", lambda e, gslot=gslot: e.tensor_tensor(
                        out=outs[gslot][:], in0=outs[gslot][:],
                        in1=xos[gslot][:], op=ALU.add), S["cmb"])
                    sl_gst.set_free(gslot, S["cmb"])
                    sl_xo.set_free(gslot, S["cmb"])
                    sl_sld.set_free(gslot, S["cmb"])
                    p.wait("gpsimd", S["cmb"])
                    p.op("gpsimd", lambda e, gslot=gslot, tg=tg, tt=tt, eh=eh:
                         e.dma_start(
                             out=y_d[tg * 1024 + tt * 128:
                                     tg * 1024 + tt * 128 + 128,
                                     eh * 512:(eh + 1) * 512],
                             in_=outs[gslot][:]), S["yst"], 16)
                    sl_out.set_free(gslot, S["yst"])
                    if bi + 4 < 8:
                        tail_load(bi + 4)

        wd_load_deq(0)
        wd_load_deq(1)
        p.wait("sync", S["gmm"])
        ffn_load_deq(0)
        ffn_load_deq(1)
        emit_down(0, preloaded=True)
        emit_gate_prologue(1, preloaded=True)
        for hc in range(24):
            mk_gate_unit(1, hc)()
            if hc == 16:
                wd_load_deq(0)
                wd_load_deq(1)
        emit_down(1, preloaded=True)
        AB.release(); AFA.release()

        p.wait("gpsimd", S["yst"])
        p.verify()

        for s in S.values():
            s.handle = ent(nc.semaphore("sem_" + s.name))
        with nc.Block() as block:
            p.emit(block)
    return nc


def split3(v):
    bf16 = ml_dtypes.bfloat16
    a1 = v.astype(np.float32).astype(bf16).astype(np.float64)
    a2 = (v - a1).astype(np.float32).astype(bf16).astype(np.float64)
    a3 = (v - a1 - a2).astype(np.float32).astype(bf16)
    return a1.astype(bf16), a2.astype(bf16), a3


def host_constants():
    bf16 = ml_dtypes.bfloat16
    q = np.arange(Q0, WIN, dtype=np.float64)
    qaug = np.zeros((H, 4, QOWN), dtype=bf16)
    for h in range(H):
        qaug[h, 0] = -((SLOPES[h] * q).astype(np.float32).astype(bf16))
        qaug[h, 1:4] = 1.0
    masks = np.zeros((4, 128, 512), np.float32)
    for j in range(4):
        for sc in range(4):
            blk = masks[j][:, sc * 128:(sc + 1) * 128]
            if sc < j:
                blk[:] = NEG
            elif sc == j:
                kk = np.arange(128)[:, None]
                qq = np.arange(128)[None, :]
                blk[:] = np.where(kk > qq, NEG, 0.0)
    ident = np.eye(128, dtype=bf16)
    return qaug, masks, ident


def kernel(L0_attn_x, L0_attn_w_in, L0_attn_w_out, L0_ffn_w_gate, L0_ffn_w_up,
           L0_ffn_w_down):
    x = np.ascontiguousarray(np.asarray(L0_attn_x, dtype=np.float32))
    w_in = np.asarray(L0_attn_w_in, dtype=np.float32)
    w_out = np.asarray(L0_attn_w_out, dtype=np.float32)
    def i8_lowbyte(a):
        a = np.ascontiguousarray(np.asarray(a, dtype="<i4"))
        return a.view(np.int8).reshape(a.shape + (4,))[..., 0]

    w_g = i8_lowbyte(np.asarray(L0_ffn_w_gate, dtype=np.int32).T)
    w_u = i8_lowbyte(np.asarray(L0_ffn_w_up, dtype=np.int32).T)
    w_d = i8_lowbyte(np.asarray(L0_ffn_w_down, dtype=np.int32).T)
    w_g = np.ascontiguousarray(
        w_g.reshape(8, 128, 24, 128).transpose(2, 1, 0, 3).reshape(24, 128, 1024))
    w_u = np.ascontiguousarray(
        w_u.reshape(8, 128, 24, 128).transpose(2, 1, 0, 3).reshape(24, 128, 1024))
    w_d = np.ascontiguousarray(
        w_d.reshape(12, 2, 128, 1024).transpose(0, 2, 1, 3).reshape(12, 128, 2048))

    qaug, masks, ident = host_constants()

    wq = w_in[0:E]
    wk = w_in[E:2 * E]
    wv = w_in[2 * E:3 * E]
    wkvp = np.zeros((NPASS, E, 512), np.float32)
    wqp = np.zeros((NPASS, E, 256), np.float32)
    for pp in range(NPASS):
        hs = slice(pp * 256, pp * 256 + 256)
        wkvp[pp, :, 0:256] = wk[hs].T
        wkvp[pp, :, 256:512] = wv[hs].T
        wqp[pp] = wq[hs].T
    wop = np.ascontiguousarray(w_out.T)

    bf16 = ml_dtypes.bfloat16
    ka_base = np.zeros((NPASS, 8, 3, HPP * 1024), dtype=bf16)
    kw = np.arange(WIN, dtype=np.float64).reshape(8, 1024)
    for pp in range(NPASS):
        for hl in range(HPP):
            sl = SLOPES[pp * HPP + hl]
            for c in range(8):
                a1, a2, a3 = split3(sl * kw[c])
                ka_base[pp, c, 0, hl * 1024:(hl + 1) * 1024] = a1
                ka_base[pp, c, 1, hl * 1024:(hl + 1) * 1024] = a2
                ka_base[pp, c, 2, hl * 1024:(hl + 1) * 1024] = a3

    nc = build_nc()
    in_maps = []
    for core in range(NCORES):
        b, qq = core // 4, core % 4
        q_end = (qq + 1) * QOWN
        pad = WIN - q_end
        xT_win = np.zeros((E, WIN), np.float32)
        xT_win[:, pad:] = x[b, 0:q_end].T
        xo = np.ascontiguousarray(x[b, qq * QOWN:q_end])
        ka = ka_base.copy()
        if pad > 0:
            kmask = np.arange(WIN).reshape(8, 1024) < pad
            for c in range(8):
                rows = kmask[c]
                if rows.any():
                    for hl in range(HPP):
                        blk = slice(hl * 1024, hl * 1024 + 1024)
                        a = ka[:, c, 0, blk]
                        a[:, rows] = NEG
                        ka[:, c, 0, blk] = a
                        ka[:, c, 1, blk][:, rows] = 0.0
                        ka[:, c, 2, blk][:, rows] = 0.0
        in_maps.append({
            "xT": xT_win, "xo": xo, "kac": ka, "qaug": qaug, "masks": masks,
            "identc": ident, "wkvp": wkvp, "wqp": wqp, "wop": wop,
            "wgp": w_g, "wup": w_u, "wdp": w_d,
        })

    res = run_bass_kernel_spmd(nc, in_maps, core_ids=list(range(NCORES)))
    out = np.empty((B, L, E), np.float32)
    for core in range(NCORES):
        b, qq = core // 4, core % 4
        out[b, qq * QOWN:(qq + 1) * QOWN] = res.results[core]["y"]
    return out


if __name__ == "__main__":
    build_nc()
    print("build ok")


# version 70
# speedup vs baseline: 1.1853x; 1.0074x over previous
import contextlib
import numpy as np
import ml_dtypes
import concourse.bass as bass
import concourse.mybir as mybir
from concourse.bass_utils import run_bass_kernel_spmd

F32 = mybir.dt.float32
BF16 = mybir.dt.bfloat16
I32 = mybir.dt.int32
I8 = mybir.dt.int8
AF = mybir.ActivationFunctionType
ALU = mybir.AluOpType

B, L, E, H, D = 2, 8192, 1024, 16, 64
HID = 3072
NCORES = 8
WIN = 8192
QOWN = 2048
Q0 = WIN - QOWN
LN_EPS = 1e-5
DEQ = 1.0 / 127.0
NEG = -1.0e30
BAND = 130.0
NPASS, HPP = 4, 4
SLOPES = [2.0 ** (-(h + 1) / 2.0) for h in range(H)]

ENGINES = ("sync", "gpsimd", "tensor", "scalar", "vector")


class Sem:
    def __init__(self, name):
        self.name = name
        self.handle = None
        self.n = 0


class Prog:
    def __init__(self):
        self.items = {e: [] for e in ENGINES}

    def op(self, eng, fn, sem=None, inc=1):
        if sem is None:
            self.items[eng].append(("op", fn, None, 0))
        else:
            self.items[eng].append(("op", fn, sem, inc))
            sem.n += inc

    def wait(self, eng, sem, val=None):
        v = sem.n if val is None else val
        if v > 0:
            self.items[eng].append(("wait", None, sem, v))

    def verify(self):
        ptr = {e: 0 for e in ENGINES}
        vals = {}
        progress = True
        while progress:
            progress = False
            for e in ENGINES:
                while ptr[e] < len(self.items[e]):
                    kind, fn, sem, v = self.items[e][ptr[e]]
                    if kind == "wait" and vals.get(id(sem), 0) < v:
                        break
                    if kind == "op" and sem is not None:
                        vals[id(sem)] = vals.get(id(sem), 0) + v
                    ptr[e] += 1
                    progress = True
        stuck = {e: ptr[e] for e in ENGINES if ptr[e] < len(self.items[e])}
        if stuck:
            lines = []
            for e, pt in stuck.items():
                kind, fn, sem, v = self.items[e][pt]
                lines.append(f"  {e}@{pt}/{len(self.items[e])}: wait "
                             f"{sem.name} >= {v} (have {vals.get(id(sem), 0)})")
            raise RuntimeError("DEADLOCK:\n" + "\n".join(lines))

    def emit(self, block):
        def mk(eng_name):
            def inner(eng):
                for kind, fn, sem, v in self.items[eng_name]:
                    if kind == "wait":
                        eng.wait_ge(sem.handle, v)
                    else:
                        inst = fn(eng)
                        if sem is not None:
                            inst.then_inc(sem.handle, v)
            return inner
        for e in ENGINES:
            getattr(block, e)(mk(e))


class Slots:
    def __init__(self, n):
        self.rec = [None] * n

    def wait_free(self, p, eng, i):
        r = self.rec[i]
        if r is not None:
            for sem, v in r:
                p.wait(eng, sem, v)

    def set_free(self, i, *sems):
        self.rec[i] = [(sem, sem.n) for sem in sems]


def band_lo_kt(h_global, q0):
    band = BAND / SLOPES[h_global]
    return max(0, int(np.floor((q0 - band) / 128.0)))


def build_nc():
    nc = bass.Bass("TRN2", target_bir_lowering=False, debug=False)

    xT_d = nc.dram_tensor("xT", [E, WIN], F32, kind="ExternalInput").ap()
    xo_d = nc.dram_tensor("xo", [QOWN, E], F32, kind="ExternalInput").ap()
    ka_d = nc.dram_tensor("kac", [NPASS, 8, 3, HPP * 1024], BF16, kind="ExternalInput").ap()
    qaug_d = nc.dram_tensor("qaug", [H, 4, QOWN], BF16, kind="ExternalInput").ap()
    masks_d = nc.dram_tensor("masks", [4, 128, 512], F32, kind="ExternalInput").ap()
    id_d = nc.dram_tensor("identc", [128, 128], BF16, kind="ExternalInput").ap()
    wkv_d = nc.dram_tensor("wkvp", [NPASS, E, 512], F32, kind="ExternalInput").ap()
    wq_d = nc.dram_tensor("wqp", [NPASS, E, 256], F32, kind="ExternalInput").ap()
    wo_d = nc.dram_tensor("wop", [E, E], F32, kind="ExternalInput").ap()
    wg_d = nc.dram_tensor("wgp", [24, 128, 1024], I8, kind="ExternalInput").ap()
    wu_d = nc.dram_tensor("wup", [24, 128, 1024], I8, kind="ExternalInput").ap()
    wd_d = nc.dram_tensor("wdp", [12, 128, 2048], I8, kind="ExternalInput").ap()
    y_d = nc.dram_tensor("y", [QOWN, E], F32, kind="ExternalOutput").ap()

    xn_img = nc.dram_tensor("xn_img", [E, WIN], BF16).ap()
    ctxT_img = nc.dram_tensor("ctxT_img", [E, QOWN], BF16).ap()
    s_img = nc.dram_tensor("s_img", [QOWN, E], F32).ap()

    with contextlib.ExitStack() as stack:
        ent = stack.enter_context

        ab = ent(nc.sbuf_tensor("ab", [128, 57216], BF16))
        af = ent(nc.sbuf_tensor("af", [128, 20608], F32))

        class Arena:
            def __init__(self, t):
                self.t, self.off, self.marks = t, 0, []

            def alloc(self, cols, part=128):
                a = self.off
                self.off += cols
                assert self.off <= self.t.shape[1], (self.t.shape[1], self.off)
                return self.t[0:part, a:a + cols]

            def mark(self):
                self.marks.append(self.off)

            def release(self):
                self.off = self.marks.pop()

        AB, AFA = Arena(ab), Arena(af)

        ident = AB.alloc(128)
        ones_col = AB.alloc(1)
        ones_row = AB.alloc(128, part=1)
        masks = [AFA.alloc(512) for _ in range(4)]
        ctxa = AFA.alloc(4160)
        rec = AFA.alloc(16)
        ctxn = [AB.alloc(256) for _ in range(2)]
        ctt = [AB.alloc(128) for _ in range(8)]

        AB.mark(); AFA.mark()
        xt = [AFA.alloc(512) for _ in range(8)]
        xtq = AFA.alloc(2048)
        ones_rf = AFA.alloc(128, part=1)
        mbc_f = AFA.alloc(512)
        rbc_f = AFA.alloc(512)
        t1f = AFA.alloc(512)
        t1g = AFA.alloc(512)
        strow = AFA.alloc(2048, part=1)
        srow2 = AFA.alloc(512, part=1)
        xb16 = [AB.alloc(4096) for _ in range(2)]
        sq16 = AB.alloc(4096)
        xn16 = [AB.alloc(4096) for _ in range(2)]
        wkv_sb = AB.alloc(4096)
        wq_sb = AB.alloc(2048)
        K_aug = [AB.alloc(4096, part=96) for _ in range(2)]
        V_aug = [AB.alloc(2080) for _ in range(2)]
        Q_aug = AB.alloc(8192, part=96)
        kqst = [AB.alloc(512) for _ in range(4)]
        PT = [AB.alloc(1024) for _ in range(6)]

        def ctx_acc(hl, su):
            base = hl * 16 + su
            return ctxa[:, base * 65:base * 65 + 65]

        ps_s = [ent(nc.psum_tensor(f"ps_s{i}", [128, 1024], F32)) for i in range(2)]
        ps_c = [ent(nc.psum_tensor(f"ps_c{i}", [128, 512], F32)) for i in range(2)]
        ps_p = ent(nc.psum_tensor("ps_p", [128, 512], F32))
        ps_l = ent(nc.psum_tensor("ps_l", [128, 512], F32))
        ps_lt = ps_l[0:2, :]
        ps_tp = ps_l.bitcast(BF16)[:, 0:128]

        sem_names = [
            "ini", "xld0", "xld1", "xld2", "xld3", "xld4", "xld5", "xld6", "xld7", "cast", "sq", "stat", "row", "rstd", "bcm", "bce",
            "app", "xnl", "xns", "prj", "pev", "krs0", "krs1", "qrs6", "qrs7", "kb0", "kb1", "wld", "qau",
            "sc", "msk", "exp", "ctx", "cev", "recs", "nrm", "tp", "tpe", "cts",
            "qld", "ctl", "ctw0", "ctw1", "wol", "smm", "sev", "sst", "sld0", "sld1", "fxl", "wgl0", "wgl1", "wgl2", "wul0", "wul1", "wul2", "wdl0", "wdl1", "wdl2", "dqw", "gmm", "sgl",
            "hml", "dmm", "gsl", "xol0", "xol1", "xol2", "xol3", "sld2", "sld3", "cmb", "yst",
        ]
        S = {n: Sem(n) for n in sem_names}
        p = Prog()

        p.op("gpsimd", lambda e: e.dma_start(out=ident[:], in_=id_d),
             S["wld"], 16)
        for j in range(4):
            p.op("gpsimd", lambda e, j=j: e.dma_start(
                out=masks[j][:], in_=masks_d[j]), S["wld"], 16)
        for bi in range(2):
            p.op("vector", lambda e, bi=bi: e.memset(K_aug[bi][64:96, :], 0.0))
            p.op("vector", lambda e, bi=bi: e.memset(K_aug[bi][64:65, :], 1.0))
            p.op("vector", lambda e, bi=bi: e.memset(
                V_aug[bi][:].rearrange("p (k h c) -> p k h c", h=HPP, c=65)[:, :, :, 64:65],
                1.0), S["ini"])
        p.op("vector", lambda e: e.memset(Q_aug[64:96, :], 0.0), S["ini"])
        p.op("vector", lambda e: e.memset(ones_col[:], 1.0), S["ini"])
        p.op("vector", lambda e: e.memset(ones_row[:], 1.0), S["ini"])
        p.op("vector", lambda e: e.memset(ones_rf[:], 1.0), S["ini"])

        sl_KV0 = Slots(2)
        sl_KV0.set_free(0, S["ini"])
        sl_KV0.set_free(1, S["ini"])
        sl_xt = Slots(8)
        sl_xb = Slots(2)
        sl_xn = Slots(2)
        sl_kq = Slots(4)
        sl_KV = Slots(2)
        sl_PT = Slots(6)
        sl_ss = Slots(2)
        sl_cs = Slots(2)
        sl_pj = Slots(2)
        sl_ctxn = Slots(2)
        sl_ctt = Slots(8)

        tile_i = [0]
        xn_done = set()

        def pass_lo_chunk(pp):
            return max(0, band_lo_kt(pp * HPP + HPP - 1, Q0) // 8)

        def emit_ln_fillers(t, c, half, hidx, xns):
            xbs = xb16[hidx % 2]
            sls = [(tile_i[0] + eg) % 8 for eg in range(8)]
            tile_i[0] += 8
            app_cell = [None]

            def L0():
                for eg in range(8):
                    sl = sls[eg]
                    sl_xt.wait_free(p, "sync", sl)
                    p.op("sync", lambda e, eg=eg, c=c, half=half, sl=sl:
                         e.dma_start(
                             out=xt[sl][:],
                             in_=xT_d[eg * 128:(eg + 1) * 128,
                                      c * 1024 + half * 512:
                                      c * 1024 + half * 512 + 512]),
                         S[f"xld{sl}"], 16)

            def L1():
                sl_xb.wait_free(p, "vector", hidx % 2)
                for eg in range(4):
                    sl = sls[eg]
                    p.wait("vector", S[f"xld{sl}"])
                    p.op("vector", lambda e, eg=eg, xbs=xbs, sl=sl:
                         e.tensor_copy(xbs[:, eg * 512:(eg + 1) * 512],
                                       xt[sl][:]), S["cast"])

            def L2():
                for eg in range(4, 8):
                    sl = sls[eg]
                    p.wait("vector", S[f"xld{sl}"])
                    p.op("vector", lambda e, eg=eg, xbs=xbs, sl=sl:
                         e.tensor_copy(xbs[:, eg * 512:(eg + 1) * 512],
                                       xt[sl][:]), S["cast"])
                p.op("vector", lambda e, xbs=xbs: e.tensor_tensor(
                    out=sq16[:], in0=xbs[:], in1=xbs[:], op=ALU.mult), S["sq"])

            def L3():
                sl_pj.wait_free(p, "tensor", 0)
                sl_pj.wait_free(p, "tensor", 1)
                p.wait("tensor", S["cast"])
                p.wait("tensor", S["ini"])
                for eg in range(8):
                    p.op("tensor", lambda e, eg=eg, xbs=xbs: e.matmul(
                        ps_p[0:1, :], ones_col[:],
                        xbs[:, eg * 512:(eg + 1) * 512],
                        start=(eg == 0), stop=(eg == 7)),
                        S["stat"] if eg == 7 else None)
                p.wait("tensor", S["sq"])
                for eg in range(8):
                    p.op("tensor", lambda e, eg=eg: e.matmul(
                        ps_l[0:1, :], ones_col[:],
                        sq16[:, eg * 512:(eg + 1) * 512],
                        start=(eg == 0), stop=(eg == 7)),
                        S["stat"] if eg == 7 else None)
                p.wait("vector", S["stat"])
                p.op("vector", lambda e: e.tensor_scalar_mul(
                    out=strow[0:1, 0:512], in0=ps_p[0:1, :], scalar1=1.0 / E))
                p.op("vector", lambda e: e.tensor_scalar_mul(
                    out=strow[0:1, 512:1024], in0=ps_l[0:1, :],
                    scalar1=1.0 / E), S["row"])
                p.op("vector", lambda e: e.tensor_tensor(
                    out=strow[0:1, 1024:1536], in0=strow[0:1, 0:512],
                    in1=strow[0:1, 0:512], op=ALU.mult))
                p.op("vector", lambda e: e.tensor_tensor(
                    out=strow[0:1, 1536:2048], in0=strow[0:1, 512:1024],
                    in1=strow[0:1, 1024:1536], op=ALU.subtract))
                p.op("vector", lambda e: e.tensor_scalar_add(
                    out=strow[0:1, 1536:2048], in0=strow[0:1, 1536:2048],
                    scalar1=LN_EPS), S["row"])

            def L4():
                p.wait("scalar", S["row"])
                p.op("scalar", lambda e: e.activation(
                    out=srow2[0:1, :], in_=strow[0:1, 1536:2048], func=AF.Ln))
                p.op("scalar", lambda e: e.activation(
                    out=srow2[0:1, :], in_=srow2[0:1, :], func=AF.Exp,
                    scale=-0.5), S["rstd"])
                p.wait("tensor", S["row"])
                p.op("tensor", lambda e: e.matmul(
                    ps_p[:], ones_rf[:], strow[0:1, 0:512], start=True,
                    stop=True), S["bcm"])
                p.wait("vector", S["bcm"])
                p.op("vector", lambda e: e.tensor_copy(mbc_f[:], ps_p[:]),
                     S["bce"])
                sl_pj.set_free(0, S["bce"])
                p.wait("tensor", S["rstd"])
                p.op("tensor", lambda e: e.matmul(
                    ps_l[:], ones_rf[:], srow2[0:1, :], start=True,
                    stop=True), S["bcm"])
                p.wait("vector", S["bcm"])
                p.op("vector", lambda e: e.tensor_copy(rbc_f[:], ps_l[:]),
                     S["bce"])
                sl_pj.set_free(1, S["bce"])

            def L5():
                sl_xn.wait_free(p, "vector", hidx % 2)
                for eg in range(8):
                    sl = sls[eg]
                    p.op("vector", lambda e, sl=sl: e.tensor_tensor(
                        out=t1f[:], in0=xt[sl][:], in1=mbc_f[:],
                        op=ALU.subtract))
                    p.op("vector", lambda e, eg=eg, xns=xns: e.tensor_tensor(
                        out=xns[:, eg * 512:(eg + 1) * 512], in0=t1f[:],
                        in1=rbc_f[:], op=ALU.mult), S["app"])
                    sl_xt.set_free(sl, S["app"])
                sl_xb.set_free(hidx % 2, S["app"])
                app_cell[0] = S["app"].n

            def L6():
                p.wait("gpsimd", S["app"], app_cell[0])
                for eg in range(8):
                    p.op("gpsimd", lambda e, eg=eg, c=c, half=half, xns=xns:
                         e.dma_start(
                             out=xn_img[eg * 128:(eg + 1) * 128,
                                        c * 1024 + half * 512:
                                        c * 1024 + half * 512 + 512],
                             in_=xns[:, eg * 512:(eg + 1) * 512]),
                         S["xns"], 16)

            return [L0, L1, L2, L3, L4, L5, L6], app_cell

        def emit_xn_load(c, half, hidx, xns):
            p.wait("sync", S["xns"])
            for eg in range(8):
                p.op("sync", lambda e, eg=eg, c=c, half=half, xns=xns: e.dma_start(
                    out=xns[:, eg * 512:(eg + 1) * 512],
                    in_=xn_img[eg * 128:(eg + 1) * 128,
                               c * 1024 + half * 512: c * 1024 + half * 512 + 512]),
                    S["xnl"], 16)

        def emit_ln_proj(pp, t, c, att, zip_q=True):
            own = c >= 6
            q_fillers = []
            do_ln = c not in xn_done
            if do_ln:
                xn_done.add(c)
            for half in range(2):
                hidx = 2 * t + half
                xns = xn16[hidx % 2]
                ln_fills, app_cell = [], None
                if do_ln:
                    ln_fills, app_cell = emit_ln_fillers(t, c, half, hidx, xns)
                elif half == 0:
                    for h2 in range(2):
                        sl_xn.wait_free(p, "sync", (hidx + h2) % 2)
                        emit_xn_load(c, h2, hidx + h2, xn16[(hidx + h2) % 2])
                kv_buf = c % 2
                if half == 0:
                    sl_KV.wait_free(p, "tensor", kv_buf)
                xnl_n = S["xnl"].n if not do_ln else None
                grp = 0

                def dep_wait(app_cell=app_cell):
                    if app_cell is not None:
                        p.wait("tensor", S["app"], app_cell[0])
                    if xnl_n is not None:
                        p.wait("tensor", S["xnl"], xnl_n)

                fillers = list(ln_fills)

                def mk_kproj(mg, xns=xns, kv_buf=kv_buf, half=half,
                             wg=None):
                    wg = wld_goal[0]
                    def fn_a():
                        pj = [ps_p, ps_l][mg % 2]
                        p.wait("tensor", S["wld"], wg)
                        dep_wait()
                        sl_pj.wait_free(p, "tensor", mg % 2)
                        sl_kq.wait_free(p, "tensor", mg)
                        for eg in range(4):
                            p.op("tensor", lambda e, eg=eg, mg=mg, xns=xns, pj=pj:
                                 e.matmul(
                                     pj[:],
                                     wkv_sb[:, eg * 512 + mg * 128: eg * 512 + mg * 128 + 128],
                                     xns[:, eg * 512:(eg + 1) * 512],
                                     start=(eg == 0), stop=False), None)
                    def fn():
                        stg = kqst[mg]
                        pj = [ps_p, ps_l][mg % 2]
                        for eg in range(4, 8):
                            p.op("tensor", lambda e, eg=eg, mg=mg, xns=xns, pj=pj:
                                 e.matmul(
                                     pj[:],
                                     wkv_sb[:, eg * 512 + mg * 128: eg * 512 + mg * 128 + 128],
                                     xns[:, eg * 512:(eg + 1) * 512],
                                     start=False, stop=(eg == 7)),
                                 S["prj"] if eg == 7 else None)
                        p.wait("vector", S["prj"])
                        p.op("vector", lambda e, stg=stg, pj=pj: e.tensor_copy(
                            stg[:], pj[:]), S["pev"])
                        sl_pj.set_free(mg % 2, S["pev"])
                        p.wait("gpsimd", S["pev"])
                        skr = S[f"krs{kv_buf}"]
                        for hh in range(2):
                            hloc = 2 * mg + hh
                            p.op("gpsimd", lambda e, stg=stg, hh=hh, hloc=hloc,
                                 half=half, kv_buf=kv_buf: e.dma_start(
                                     out=K_aug[kv_buf][0:64,
                                                       hloc * 1024 + half * 512:
                                                       hloc * 1024 + half * 512 + 512],
                                     in_=stg[64 * hh:64 * hh + 64, :]), skr, 16)
                        sl_kq.set_free(mg, skr)
                    return [fn_a, fn]

                def mk_vproj(tt, xns=xns, kv_buf=kv_buf, half=half):
                    wg = wld_goal[0]
                    def fn_a():
                        pj = [ps_p, ps_l][tt % 2]
                        p.wait("tensor", S["wld"], wg)
                        dep_wait()
                        sl_pj.wait_free(p, "tensor", tt % 2)
                        for eg in range(4):
                            p.op("tensor", lambda e, tt=tt, eg=eg, xns=xns, pj=pj:
                                 e.matmul(
                                     pj[:, 0:256],
                                     xns[:, eg * 512 + tt * 128: eg * 512 + tt * 128 + 128],
                                     wkv_sb[:, eg * 512 + 256: eg * 512 + 512],
                                     start=(eg == 0), stop=False), None)
                    def fn():
                        ktl = 4 * half + tt
                        pj = [ps_p, ps_l][tt % 2]
                        for eg in range(4, 8):
                            p.op("tensor", lambda e, tt=tt, eg=eg, xns=xns, pj=pj:
                                 e.matmul(
                                     pj[:, 0:256],
                                     xns[:, eg * 512 + tt * 128: eg * 512 + tt * 128 + 128],
                                     wkv_sb[:, eg * 512 + 256: eg * 512 + 512],
                                     start=False, stop=(eg == 7)),
                                 S["prj"] if eg == 7 else None)
                        p.wait("vector", S["prj"])
                        p.op("vector", lambda e, ktl=ktl, kv_buf=kv_buf, pj=pj:
                             e.tensor_copy(
                                 V_aug[kv_buf][:, ktl * 260: ktl * 260 + 260]
                                 .rearrange("p (h c) -> p h c", c=65)[:, :, 0:64],
                                 pj[:, 0:256].rearrange("p (h c) -> p h c", c=64)),
                             S["pev"])
                        sl_pj.set_free(tt % 2, S["pev"])
                    return [fn_a, fn]

                def mk_qproj(mg, xns=xns, c=c, half=half):
                    wg = wld_goal[0]
                    def fn_a():
                        pj = [ps_p, ps_l][mg % 2]
                        p.wait("tensor", S["wld"], wg)
                        dep_wait()
                        sl_pj.wait_free(p, "tensor", mg % 2)
                        sl_kq.wait_free(p, "tensor", 2 + mg)
                        for eg in range(4):
                            p.op("tensor", lambda e, mg=mg, eg=eg, xns=xns, pj=pj:
                                 e.matmul(
                                     pj[:],
                                     wq_sb[:, eg * 256 + mg * 128: eg * 256 + mg * 128 + 128],
                                     xns[:, eg * 512:(eg + 1) * 512],
                                     start=(eg == 0), stop=False), None)
                    def fn():
                        stg = kqst[2 + mg]
                        pj = [ps_p, ps_l][mg % 2]
                        for eg in range(4, 8):
                            p.op("tensor", lambda e, mg=mg, eg=eg, xns=xns, pj=pj:
                                 e.matmul(
                                     pj[:],
                                     wq_sb[:, eg * 256 + mg * 128: eg * 256 + mg * 128 + 128],
                                     xns[:, eg * 512:(eg + 1) * 512],
                                     start=False, stop=(eg == 7)),
                                 S["prj"] if eg == 7 else None)
                        p.wait("vector", S["prj"])
                        p.op("vector", lambda e, stg=stg, pj=pj: e.tensor_scalar_mul(
                            out=stg[:], in0=pj[:], scalar1=0.125), S["pev"])
                        sl_pj.set_free(mg % 2, S["pev"])
                        p.wait("gpsimd", S["pev"])
                        qcol = (c - 6) * 1024 + half * 512
                        sqr = S[f"qrs{c}"]
                        for hh in range(2):
                            hloc = 2 * mg + hh
                            p.op("gpsimd", lambda e, stg=stg, hh=hh, hloc=hloc,
                                 qcol=qcol: e.dma_start(
                                     out=Q_aug[0:64, hloc * 2048 + qcol:
                                               hloc * 2048 + qcol + 512],
                                     in_=stg[64 * hh:64 * hh + 64, :]), sqr, 16)
                        sl_kq.set_free(2 + mg, sqr)
                    return [fn_a, fn]

                for mg in range(2):
                    fillers.extend(mk_kproj(mg))
                for tt in range(4):
                    fillers.extend(mk_vproj(tt))
                if own:
                    for mg in range(2):
                        (fillers if zip_q else q_fillers).extend(mk_qproj(mg))

                def post_fn(hidx=hidx, do_ln=do_ln):
                    if do_ln:
                        sl_xn.set_free(hidx % 2, S["prj"], S["xns"])
                    else:
                        sl_xn.set_free(hidx % 2, S["prj"])
                (fillers if zip_q else q_fillers).append(post_fn)

                if att is not None:
                    emit_attention(att[0], att[1], att[2],
                                   [2 * half, 2 * half + 1], fillers)
                for fn in fillers:
                    fn()
                fillers.clear()
                for fn in q_fillers:
                    qproj_pending.append(fn)
                q_fillers = []
            kv_buf = c % 2
            sl_KV0.wait_free(p, "sync", kv_buf)
            sl_KV.wait_free(p, "sync", kv_buf)
            p.op("sync", lambda e, pp=pp, c=c, kv_buf=kv_buf: e.dma_start(
                out=K_aug[kv_buf][65:68, :], in_=ka_d[pp, c]), S[f"kb{kv_buf}"], 16)
            return {"krs": S[f"krs{kv_buf}"].n, "pev": S["pev"].n,
                    "kb": S[f"kb{kv_buf}"].n, "qrs6": S["qrs6"].n,
                    "qrs7": S["qrs7"].n}

        pair_i = [0]
        pending = []

        def emit_attention(pp, c, snap, heads, fillers=None):
            kv_buf = c % 2
            fill_ctr = [0]
            p.wait("tensor", S[f"krs{kv_buf}"], snap["krs"])
            p.wait("tensor", S[f"kb{kv_buf}"], snap["kb"])
            p.wait("tensor", S["pev"], snap["pev"])
            p.wait("tensor", S["qrs6"], snap["qrs6"])
            p.wait("tensor", S["qrs7"], snap["qrs7"])
            p.wait("tensor", S["qau"])
            p.wait("tensor", S["ini"])
            p.wait("vector", S["wld"], 80)
            for hl in heads:
                hg = pp * HPP + hl
                for m in range(4):
                    q0 = Q0 + 512 * m
                    hi_kt = min(8 * c + 7, (q0 + 511) // 128)
                    lo_kt = max(8 * c, band_lo_kt(hg, q0))
                    if lo_kt > hi_kt:
                        continue
                    kts = list(range(lo_kt, hi_kt + 1))
                    cps = ps_c[m % 2]
                    def su_live(su, kt, q0=q0, hg=hg):
                        squ0 = q0 + su * 128
                        if kt * 128 > squ0 + 127:
                            return False
                        if SLOPES[hg] * (squ0 - (kt * 128 + 127)) >= BAND:
                            return False
                        return True
                    live_mms = [(kt, su) for kt in kts for su in range(4)
                                if su_live(su, kt)]
                    if not live_mms:
                        continue
                    n_mm = len(live_mms)
                    su_set = sorted({su for _, su in live_mms})
                    mm_i = 0
                    pairs = [kts[i:i + 2] for i in range(0, len(kts), 2)]
                    npairs = len(pairs)
                    for pidx, pr in enumerate(pairs):
                        fill_ctr[0] += 1
                        if fillers and fill_ctr[0] >= 2:
                            fillers.pop(0)()
                        sb = pair_i[0] % 2
                        ss = ps_s[sb]
                        sl_ss.wait_free(p, "tensor", sb)
                        offs = []
                        wrs = []
                        for j, kt in enumerate(pr):
                            ktl = kt - 8 * c
                            dj = kt - q0 // 128
                            off = 128 * dj if dj > 0 else 0
                            wr = 128 * (1 + max(su for su in range(4)
                                                if su_live(su, kt)))
                            offs.append(off)
                            wrs.append(wr)
                            p.op("tensor", lambda e, ss=ss, j=j, hl=hl, ktl=ktl,
                                 m=m, kv_buf=kv_buf, off=off, wr=wr: e.matmul(
                                     ss[:, 512 * j + off:512 * j + wr],
                                     K_aug[kv_buf][:, hl * 1024 + ktl * 128:
                                                   hl * 1024 + ktl * 128 + 128],
                                     Q_aug[:, hl * 2048 + m * 512 + off:
                                           hl * 2048 + m * 512 + wr],
                                     start=True, stop=True), S["sc"])
                        nmask = 0
                        for j, kt in enumerate(pr):
                            dj = kt - q0 // 128
                            if dj >= 0:
                                p.wait("vector", S["sc"])
                                p.op("vector", lambda e, ss=ss, j=j, dj=dj:
                                     e.tensor_tensor(
                                         out=ss[:, 512 * j + 128 * dj:
                                                512 * j + 128 * dj + 128],
                                         in0=ss[:, 512 * j + 128 * dj:
                                                512 * j + 128 * dj + 128],
                                         in1=masks[dj][:, 128 * dj:128 * dj + 128],
                                         op=ALU.add), S["msk"])
                                nmask += 1
                        pts = PT[pair_i[0] % 6]
                        sl_PT.wait_free(p, "scalar", pair_i[0] % 6)
                        p.wait("scalar", S["sc"])
                        if nmask:
                            p.wait("scalar", S["msk"])
                        w = 512 * (len(pr) - 1) + wrs[-1]
                        o0 = offs[0]
                        p.op("scalar", lambda e, pts=pts, ss=ss, w=w, o0=o0:
                             e.activation(out=pts[:, o0:w], in_=ss[:, o0:w],
                                          func=AF.Exp), S["exp"])
                        sl_ss.set_free(sb, S["exp"])

                        mm_specs = []
                        for j, kt in enumerate(pr):
                            for su in range(4):
                                if not su_live(su, kt):
                                    continue
                                mm_specs.append((j, kt, su, mm_i == 0,
                                                 mm_i == n_mm - 1))
                                mm_i += 1
                        is_last_pair = (pidx == npairs - 1)
                        is_first_pair = (pidx == 0)
                        my_slot = pair_i[0] % 6
                        exp_n = S["exp"].n

                        def ctx_fn(mm_specs=mm_specs, cps=cps, kv_buf=kv_buf,
                                   hl=hl, m=m, c=c, pts=pts, my_slot=my_slot,
                                   exp_n=exp_n, is_last_pair=is_last_pair,
                                   is_first_pair=is_first_pair,
                                   su_set=su_set, q0=q0):
                            if is_first_pair:
                                sl_cs.wait_free(p, "tensor", m % 2)
                            p.wait("tensor", S["exp"], exp_n)
                            for j, kt, su, first, last in mm_specs:
                                p.op("tensor", lambda e, cps=cps, su=su, j=j,
                                     kt=kt, c=c, kv_buf=kv_buf, hl=hl, pts=pts,
                                     first=first, last=last: e.matmul(
                                         cps[:, su * 65: su * 65 + 65],
                                         pts[:, 512 * j + su * 128:
                                             512 * j + su * 128 + 128],
                                         V_aug[kv_buf][:, (kt - 8 * c) * 260 + hl * 65:
                                                       (kt - 8 * c) * 260 + hl * 65 + 65],
                                         start=first, stop=last), S["ctx"])
                            sl_PT.set_free(my_slot, S["ctx"])
                            if is_last_pair:
                                p.op("tensor", lambda e: e.nop(), S["ctx"])
                                p.wait("vector", S["ctx"])
                                first_ever = (c == (q0 + 511) // 1024)
                                for su in range(su_set[0], su_set[-1] + 1):
                                    acc = ctx_acc(hl, 4 * m + su)
                                    if first_ever:
                                        p.op("vector", lambda e, acc=acc, cps=cps,
                                             su=su: e.tensor_copy(
                                                 acc, cps[:, su * 65:su * 65 + 65]),
                                             S["cev"])
                                    else:
                                        p.op("vector", lambda e, acc=acc, cps=cps,
                                             su=su: e.tensor_tensor(
                                                 out=acc,
                                                 in0=cps[:, su * 65:su * 65 + 65],
                                                 in1=acc, op=ALU.add), S["cev"])
                                sl_cs.set_free(m % 2, S["cev"])

                        pending.append(ctx_fn)
                        if len(pending) > 5:
                            pending.pop(0)()
                        pair_i[0] += 1
            if heads[-1] == 3:
                while pending:
                    pending.pop(0)()
                p.op("tensor", lambda e: e.nop(), S["sc"])
                sl_KV.set_free(kv_buf, S["sc"])

        def emit_finalize(pp, zipper=None):
            p.wait("vector", S["ctx"])
            p.wait("scalar", S["cev"])
            nrm_when = {}
            prev_tp = [None]
            zi = [0]
            for su in range(16):
                if zipper:
                    while zi[0] < (su * 3) // 2 and zipper:
                        zipper.pop(0)()
                        zi[0] += 1
                if su >= 4:
                    p.wait("vector", S["nrm"], nrm_when[su - 4])
                for hl in range(HPP):
                    acc = ctx_acc(hl, su)
                    sl = (su * HPP + hl) % 16
                    p.op("vector", lambda e, acc=acc, sl=sl: e.reciprocal(
                        rec[:, sl:sl + 1], acc[:, 64:65]), S["recs"])
                cn = ctxn[su % 2]
                sl_ctxn.wait_free(p, "scalar", su % 2)
                p.wait("scalar", S["recs"])
                for hl in range(HPP):
                    acc = ctx_acc(hl, su)
                    sl = (su * HPP + hl) % 16
                    if hl < 2:
                        p.op("scalar", lambda e, acc=acc, cn=cn, hl=hl, sl=sl:
                             e.activation(out=cn[:, hl * 64:(hl + 1) * 64],
                                          in_=acc[:, 0:64], func=AF.Copy,
                                          scale=rec[:, sl:sl + 1]), S["nrm"])
                    else:
                        p.op("vector", lambda e, acc=acc, cn=cn, hl=hl, sl=sl:
                             e.tensor_scalar_mul(
                                 out=cn[:, hl * 64:(hl + 1) * 64],
                                 in0=acc[:, 0:64],
                                 scalar1=rec[:, sl:sl + 1]), S["nrm"])
                nrm_when[su] = S["nrm"].n

                def tp_fn(su=su, cn=cn, pp=pp, nrm_n=S["nrm"].n):
                    p.wait("tensor", S["nrm"], nrm_n)
                    for hp in range(2):
                        cslot = (su * 2 + hp) % 8
                        ct = ctt[cslot]
                        tb, trk, tix = [(ps_p, sl_pj, 0), (ps_l, sl_pj, 1),
                                        (ps_c[0], sl_cs, 0),
                                        (ps_c[1], sl_cs, 1)][cslot % 4]
                        tbv = tb.bitcast(BF16)[:, 0:128]
                        sl_ctt.wait_free(p, "tensor", cslot)
                        trk.wait_free(p, "tensor", tix)
                        p.op("tensor", lambda e, cn=cn, hp=hp, tbv=tbv:
                             e.transpose(tbv, cn[:, hp * 128:(hp + 1) * 128],
                                         ident[:]), S["tp"])
                        p.wait("vector", S["tp"])
                        p.op("vector", lambda e, ct=ct, tbv=tbv: e.tensor_copy(
                            ct[:], tbv), S["tpe"])
                        trk.set_free(tix, S["tpe"])
                        p.wait("gpsimd", S["tpe"])
                        p.op("gpsimd", lambda e, ct=ct, pp=pp, hp=hp, su=su:
                             e.dma_start(
                                 out=ctxT_img[pp * 256 + hp * 128:
                                              pp * 256 + (hp + 1) * 128,
                                              su * 128:(su + 1) * 128],
                                 in_=ct[:]), S["cts"], 16)
                        sl_ctt.set_free(cslot, S["cts"])
                    sl_ctxn.set_free(su % 2, S["tp"])

                if prev_tp[0] is not None:
                    prev_tp[0]()
                prev_tp[0] = tp_fn
            prev_tp[0]()


        carry = [None]
        qproj_pending = []
        def emit_wload(pp):
            p.wait("sync", S["wld"])
            p.op("sync", lambda e, pp=pp: e.dma_start(
                out=xtq[:].rearrange("p (w c) -> p w c", c=256),
                in_=wq_d[pp].rearrange("(w p) c -> p w c", p=128)),
                S["qld"], 16)
            if pp == 0:
                for wi in range(8):
                    p.op("gpsimd", lambda e, wi=wi: e.dma_start(
                        out=wkv_sb[:, wi * 512:(wi + 1) * 512],
                        in_=wkv_d[0, wi * 128:(wi + 1) * 128, :]),
                        S["wld"], 16)
                return
            for wi in range(8):
                sl_xt.wait_free(p, "sync", wi)
                p.op("sync", lambda e, pp=pp, wi=wi: e.dma_start(
                    out=xt[wi][:],
                    in_=wkv_d[pp, wi * 128:(wi + 1) * 128, :]),
                    S[f"xld{wi}"], 16)

        def emit_wcast(pp):
            p.wait("vector", S["prj"])
            if pp > 0:
                for wi in range(8):
                    p.wait("vector", S[f"xld{wi}"])
                    p.op("vector", lambda e, wi=wi: e.tensor_copy(
                        wkv_sb[:, wi * 512:(wi + 1) * 512], xt[wi][:]),
                        S["wld"])
                    sl_xt.set_free(wi, S["wld"])
            p.wait("vector", S["qld"])
            for wi in range(8):
                p.op("vector", lambda e, wi=wi: e.tensor_copy(
                    wq_sb[:, wi * 256:(wi + 1) * 256],
                    xtq[:, wi * 256:wi * 256 + 256]), S["wld"])

        wld_goal = [0]
        for pp in range(NPASS):
            if pp == 0:
                emit_wload(0)
            emit_wcast(pp)
            wld_goal[0] = S["wld"].n

            snaps = {}
            lo_c = pass_lo_chunk(pp)
            for t, c in enumerate(range(7, lo_c - 1, -1)):
                att = carry[0] if t == 0 else (pp, c + 1, snaps[c + 1])
                snaps[c] = emit_ln_proj(pp, t, c, att, zip_q=(t != 0))
                if t == 0:
                    p.wait("sync", S["sc"])
                    p.wait("sync", S["ini"])
                    for hl in range(HPP):
                        p.op("sync", lambda e, hl=hl, pp=pp: e.dma_start(
                            out=Q_aug[64:68, hl * 2048:(hl + 1) * 2048],
                            in_=qaug_d[pp * HPP + hl]), S["qau"], 16)
                    p.wait("gpsimd", S["sc"])
                    while qproj_pending:
                        qproj_pending.pop(0)()
                    snaps[7]["qrs6"] = S["qrs6"].n
                    snaps[7]["qrs7"] = S["qrs7"].n
                    snaps[7]["pev"] = S["pev"].n
                    if pp > 0:
                        emit_finalize(pp - 1)
                    p.wait("vector", S["nrm"])
            carry[0] = (pp, lo_c, snaps[lo_c])
            if pp + 1 < NPASS:
                emit_wload(pp + 1)

        emit_attention(carry[0][0], carry[0][1], carry[0][2],
                       [0, 1, 2, 3], None)

        AB.release(); AFA.release()
        AB.mark(); AFA.mark()
        h_sb = AB.alloc(24 * 1024)
        xnT_sb = AB.alloc(8192)
        wg_bf = [AB.alloc(1024) for _ in range(3)]
        wu_bf = [AB.alloc(1024) for _ in range(3)]
        wd_bf = [AB.alloc(2048) for _ in range(3)]
        sg_bf = [AB.alloc(512) for _ in range(2)]
        ctxw = [AB.alloc(1024) for _ in range(2)]
        woT_sb = xnT_sb
        gst = [AFA.alloc(512) for _ in range(4)]
        xos = [AFA.alloc(512) for _ in range(4)]
        sld = [AFA.alloc(512) for _ in range(4)]
        outs = [AFA.alloc(512) for _ in range(4)]
        sst = [gst[0], gst[1], sld[0], sld[1]]
        wg_i = [AFA.alloc(256).bitcast(I8) for _ in range(2)]
        wu_i = [AFA.alloc(256).bitcast(I8) for _ in range(2)]
        wd_i = [AFA.alloc(512).bitcast(I8) for _ in range(3)]

        sl_wgi = Slots(2)
        sl_wui = Slots(2)
        sl_wdi = Slots(3)
        sl_wgb = Slots(3)
        sl_wub = Slots(3)
        sl_wdb = Slots(3)
        sl_sg = Slots(2)
        sl_gst = Slots(4)
        sl_xo = Slots(4)
        sl_out = Slots(4)
        sl_sld = Slots(4)
        sl_sst = Slots(4)
        sl_ctw = Slots(2)
        sl_dps = Slots(8)

        down_banks = [ps_s[0][:, 0:512], ps_s[0][:, 512:1024],
                      ps_s[1][:, 0:512], ps_s[1][:, 512:1024],
                      ps_c[0][:], ps_c[1][:], ps_p[:], ps_l[:]]

        def ffn_load_deq(hc):
            gslot = hc % 3
            islot = hc % 2
            sl_wgi.wait_free(p, "sync", islot)
            p.op("sync", lambda e, hc=hc, islot=islot: e.dma_start(
                out=wg_i[islot][:], in_=wg_d[hc]), S[f"wgl{islot}"], 16)
            sl_wui.wait_free(p, "sync", islot)
            p.op("sync", lambda e, hc=hc, islot=islot: e.dma_start(
                out=wu_i[islot][:], in_=wu_d[hc]), S[f"wul{islot}"], 16)
            p.wait("scalar", S[f"wgl{islot}"])
            p.wait("scalar", S[f"wul{islot}"])
            sl_wgb.wait_free(p, "scalar", gslot)
            p.op("scalar", lambda e, gslot=gslot, islot=islot: e.activation(
                out=wg_bf[gslot][:], in_=wg_i[islot][:], func=AF.Copy),
                S["dqw"])
            sl_wgi.set_free(islot, S["dqw"])
            sl_wub.wait_free(p, "scalar", gslot)
            p.op("scalar", lambda e, gslot=gslot, islot=islot: e.activation(
                out=wu_bf[gslot][:], in_=wu_i[islot][:], func=AF.Copy),
                S["dqw"])
            sl_wui.set_free(islot, S["dqw"])

        def mk_gate_unit(tg, hc):
            def fn():
                gslot = hc % 3
                if hc + 2 < 24:
                    ffn_load_deq(hc + 2)
                p.wait("tensor", S["dqw"],
                       S["dqw"].n - (2 if hc + 2 < 24 else 0))
                p.wait("tensor", S["fxl"])
                for nh in range(2):
                    gb = down_banks[nh]
                    ub = down_banks[2 + nh]
                    sl_dps.wait_free(p, "tensor", nh)
                    sl_dps.wait_free(p, "tensor", 2 + nh)
                    for eg in range(8):
                        p.op("tensor", lambda e, gb=gb, eg=eg, nh=nh, gslot=gslot:
                             e.matmul(gb, wg_bf[gslot][:, eg * 128:(eg + 1) * 128],
                                      xnT_sb[:, eg * 1024 + nh * 512:
                                             eg * 1024 + nh * 512 + 512],
                                      start=(eg == 0), stop=(eg == 7)),
                             S["gmm"] if eg == 7 else None)
                    for eg in range(8):
                        p.op("tensor", lambda e, ub=ub, eg=eg, nh=nh, gslot=gslot:
                             e.matmul(ub, wu_bf[gslot][:, eg * 128:(eg + 1) * 128],
                                      xnT_sb[:, eg * 1024 + nh * 512:
                                             eg * 1024 + nh * 512 + 512],
                                      start=(eg == 0), stop=(eg == 7)),
                             S["gmm"] if eg == 7 else None)
                    sgs = sg_bf[nh]
                    sl_sg.wait_free(p, "scalar", nh)
                    p.wait("scalar", S["gmm"], S["gmm"].n - 1)
                    p.op("scalar", lambda e, gb=gb, sgs=sgs: e.activation(
                        out=sgs[:], in_=gb, func=AF.Silu, scale=DEQ), S["sgl"])
                    sl_dps.set_free(nh, S["sgl"])
                    p.wait("vector", S["gmm"])
                    p.wait("vector", S["sgl"])
                    p.op("vector", lambda e, ub=ub, sgs=sgs, hc=hc, nh=nh:
                         e.scalar_tensor_tensor(
                             out=h_sb[:, hc * 1024 + nh * 512:
                                      hc * 1024 + nh * 512 + 512],
                             in0=ub, scalar=DEQ, in1=sgs[:],
                             op0=ALU.mult, op1=ALU.mult), S["hml"])
                    sl_dps.set_free(2 + nh, S["hml"])
                    sl_sg.set_free(nh, S["hml"])
                sl_wgb.set_free(gslot, S["gmm"])
                sl_wub.set_free(gslot, S["gmm"])
            return fn

        def emit_gate_prologue(tg, preloaded=False):
            if tg > 0:
                p.wait("sync", S["gmm"])
                p.wait("sync", S["smm"])
            p.wait("sync", S["xns"])
            for eg in range(8):
                p.op("sync", lambda e, eg=eg, tg=tg: e.dma_start(
                    out=xnT_sb[:, eg * 1024:(eg + 1) * 1024],
                    in_=xn_img[eg * 128:(eg + 1) * 128,
                               Q0 + tg * 1024: Q0 + (tg + 1) * 1024]),
                    S["fxl"], 16)
            if not preloaded:
                ffn_load_deq(0)
                ffn_load_deq(1)

        p.wait("tensor", S["exp"])
        p.wait("tensor", S["cev"])
        for sem in ("sc", "ctx", "prj", "pev", "exp", "cev",
                    "krs0", "krs1", "qrs6", "qrs7", "app", "xnl"):
            p.wait("sync", S[sem])
        emit_gate_prologue(0)
        gate_units = [mk_gate_unit(0, hc) for hc in range(24)]
        emit_finalize(NPASS - 1, gate_units)
        while gate_units:
            gate_units.pop(0)()

        p.wait("gpsimd", S["cts"])
        p.wait("gpsimd", S["wld"])
        p.wait("gpsimd", S["gmm"])
        p.op("gpsimd", lambda e: e.dma_start(
            out=woT_sb[:].rearrange("p (a c) -> p a c", a=8),
            in_=wo_d.rearrange("(a p) c -> p a c", p=128)), S["wol"], 16)
        p.wait("tensor", S["wol"])
        p.wait("tensor", S["sgl"])
        p.wait("tensor", S["hml"])
        p.wait("sync", S["cts"])

        def ctw_load(su):
            cw = ctxw[su % 2]
            scw = S[f"ctw{su % 2}"]
            sl_ctw.wait_free(p, "sync", su % 2)
            for eg in range(8):
                p.op("sync", lambda e, eg=eg, su=su, cw=cw: e.dma_start(
                    out=cw[:, eg * 128:(eg + 1) * 128],
                    in_=ctxT_img[eg * 128:(eg + 1) * 128,
                                 su * 128:(su + 1) * 128]), scw, 16)
            return S[f"ctw{su % 2}"].n

        ctw_when = {0: ctw_load(0)}
        for su in range(16):
            cw = ctxw[su % 2]
            if su + 1 < 16:
                ctw_when[su + 1] = ctw_load(su + 1)
            spb = ps_s[su % 2]
            sl_ss.wait_free(p, "tensor", su % 2)
            p.wait("tensor", S[f"ctw{su % 2}"], ctw_when[su])
            for eh in range(2):
                for eg in range(8):
                    p.op("tensor", lambda e, spb=spb, cw=cw, eh=eh, eg=eg:
                         e.matmul(
                             spb[:, eh * 512:(eh + 1) * 512],
                             cw[:, eg * 128:(eg + 1) * 128],
                             woT_sb[:, eg * 1024 + eh * 512:
                                    eg * 1024 + eh * 512 + 512],
                             start=(eg == 0), stop=(eg == 7)),
                         S["smm"] if eg == 7 else None)
            sl_ctw.set_free(su % 2, S["smm"])
            p.wait("vector", S["smm"])
            for eh in range(2):
                sslot = (su * 2 + eh) % 4
                sl_sst.wait_free(p, "vector", sslot)
                p.op("vector", lambda e, spb=spb, eh=eh, sslot=sslot:
                     e.tensor_copy(sst[sslot][:],
                                   spb[:, eh * 512:(eh + 1) * 512]), S["sev"])
                p.wait("gpsimd", S["sev"])
                p.op("gpsimd", lambda e, su=su, eh=eh, sslot=sslot: e.dma_start(
                    out=s_img[su * 128:(su + 1) * 128, eh * 512:(eh + 1) * 512],
                    in_=sst[sslot][:]), S["sst"], 16)
                sl_sst.set_free(sslot, S["sst"])
            sl_ss.set_free(su % 2, S["sev"])

        def wd_load_deq(hcp):
            dslot = hcp % 3
            sl_wdi.wait_free(p, "sync", dslot)
            p.op("sync", lambda e, hcp=hcp, dslot=dslot: e.dma_start(
                out=wd_i[dslot][:], in_=wd_d[hcp]), S[f"wdl{dslot}"], 16)
            p.wait("scalar", S[f"wdl{dslot}"])
            sl_wdb.wait_free(p, "scalar", dslot)
            p.op("scalar", lambda e, dslot=dslot: e.activation(
                out=wd_bf[dslot][:], in_=wd_i[dslot][:], func=AF.Copy),
                S["dqw"])
            sl_wdi.set_free(dslot, S["dqw"])

        def emit_down(tg, preloaded=False):
            p.wait("tensor", S["hml"])
            p.wait("tensor", S["sev"])
            p.wait("scalar", S["sst"])
            for th in range(2):
                if th == 0 and not preloaded:
                    wd_load_deq(0)
                    wd_load_deq(1)
                for hcp in range(12):
                    dslot = hcp % 3
                    nxt = hcp + 2
                    if nxt < 12:
                        wd_load_deq(nxt)
                    elif th == 0:
                        wd_load_deq(nxt - 12)
                    p.wait("tensor", S["dqw"],
                           S["dqw"].n - (2 if (nxt < 12 or th == 0) else
                                         (1 if hcp < 11 else 0)))
                    for sub in range(2):
                        hc = 2 * hcp + sub
                        for bi in range(8):
                            eh, t4 = bi // 4, bi % 4
                            tt = th * 4 + t4
                            bk = down_banks[bi]
                            if hc == 0:
                                sl_dps.wait_free(p, "tensor", bi)
                            p.op("tensor", lambda e, bk=bk, hc=hc, tt=tt, eh=eh,
                                 dslot=dslot, sub=sub: e.matmul(
                                     bk,
                                     h_sb[:, hc * 1024 + tt * 128:
                                          hc * 1024 + tt * 128 + 128],
                                     wd_bf[dslot][:, sub * 1024 + eh * 512:
                                                  sub * 1024 + eh * 512 + 512],
                                     start=(hc == 0), stop=(hc == 23)),
                                 S["dmm"] if bi == 7 else None)
                    sl_wdb.set_free(dslot, S["dmm"])
                def tail_load(j, tg=tg, th=th):
                    gslot = j % 4
                    eh, t4 = j // 4, j % 4
                    tt = th * 4 + t4
                    sl_xo.wait_free(p, "sync", gslot)
                    p.op("sync", lambda e, tg=tg, tt=tt, gslot=gslot, eh=eh:
                         e.dma_start(
                             out=xos[gslot][:],
                             in_=xo_d[tg * 1024 + tt * 128:
                                      tg * 1024 + tt * 128 + 128,
                                      eh * 512:(eh + 1) * 512]),
                         S[f"xol{gslot}"], 16)
                    p.wait("sync", S["sst"])
                    sl_sld.wait_free(p, "sync", gslot)
                    p.op("sync", lambda e, tg=tg, tt=tt, gslot=gslot, eh=eh:
                         e.dma_start(
                             out=sld[gslot][:],
                             in_=s_img[tg * 1024 + tt * 128:
                                       tg * 1024 + tt * 128 + 128,
                                       eh * 512:(eh + 1) * 512]),
                         S[f"sld{gslot}"], 16)

                for j in range(4):
                    tail_load(j)
                for bi in range(8):
                    eh, t4 = bi // 4, bi % 4
                    tt = th * 4 + t4
                    bk = down_banks[bi]
                    gslot = bi % 4
                    sl_gst.wait_free(p, "scalar", gslot)
                    p.wait("scalar", S["dmm"])
                    p.op("scalar", lambda e, bk=bk, gslot=gslot: e.activation(
                        out=gst[gslot][:], in_=bk, func=AF.Silu, scale=DEQ),
                        S["gsl"])
                    sl_dps.set_free(bi, S["gsl"])
                    sl_out.wait_free(p, "vector", gslot)
                    p.wait("vector", S["gsl"])
                    p.wait("vector", S[f"xol{gslot}"])
                    p.wait("vector", S[f"sld{gslot}"])
                    p.op("vector", lambda e, gslot=gslot: e.tensor_tensor(
                        out=outs[gslot][:], in0=gst[gslot][:],
                        in1=sld[gslot][:], op=ALU.mult))
                    p.op("vector", lambda e, gslot=gslot: e.tensor_tensor(
                        out=outs[gslot][:], in0=outs[gslot][:],
                        in1=xos[gslot][:], op=ALU.add), S["cmb"])
                    sl_gst.set_free(gslot, S["cmb"])
                    sl_xo.set_free(gslot, S["cmb"])
                    sl_sld.set_free(gslot, S["cmb"])
                    p.wait("gpsimd", S["cmb"])
                    p.op("gpsimd", lambda e, gslot=gslot, tg=tg, tt=tt, eh=eh:
                         e.dma_start(
                             out=y_d[tg * 1024 + tt * 128:
                                     tg * 1024 + tt * 128 + 128,
                                     eh * 512:(eh + 1) * 512],
                             in_=outs[gslot][:]), S["yst"], 16)
                    sl_out.set_free(gslot, S["yst"])
                    if bi + 4 < 8:
                        tail_load(bi + 4)

        wd_load_deq(0)
        wd_load_deq(1)
        p.wait("sync", S["gmm"])
        ffn_load_deq(0)
        ffn_load_deq(1)
        emit_down(0, preloaded=True)
        emit_gate_prologue(1, preloaded=True)
        for hc in range(24):
            mk_gate_unit(1, hc)()
            if hc == 16:
                wd_load_deq(0)
                wd_load_deq(1)
        emit_down(1, preloaded=True)
        AB.release(); AFA.release()

        p.wait("gpsimd", S["yst"])
        p.verify()

        for s in S.values():
            s.handle = ent(nc.semaphore("sem_" + s.name))
        with nc.Block() as block:
            p.emit(block)
    return nc


def split3(v):
    bf16 = ml_dtypes.bfloat16
    a1 = v.astype(np.float32).astype(bf16).astype(np.float64)
    a2 = (v - a1).astype(np.float32).astype(bf16).astype(np.float64)
    a3 = (v - a1 - a2).astype(np.float32).astype(bf16)
    return a1.astype(bf16), a2.astype(bf16), a3


def host_constants():
    bf16 = ml_dtypes.bfloat16
    q = np.arange(Q0, WIN, dtype=np.float64)
    qaug = np.zeros((H, 4, QOWN), dtype=bf16)
    for h in range(H):
        qaug[h, 0] = -((SLOPES[h] * q).astype(np.float32).astype(bf16))
        qaug[h, 1:4] = 1.0
    masks = np.zeros((4, 128, 512), np.float32)
    for j in range(4):
        for sc in range(4):
            blk = masks[j][:, sc * 128:(sc + 1) * 128]
            if sc < j:
                blk[:] = NEG
            elif sc == j:
                kk = np.arange(128)[:, None]
                qq = np.arange(128)[None, :]
                blk[:] = np.where(kk > qq, NEG, 0.0)
    ident = np.eye(128, dtype=bf16)
    return qaug, masks, ident


def kernel(L0_attn_x, L0_attn_w_in, L0_attn_w_out, L0_ffn_w_gate, L0_ffn_w_up,
           L0_ffn_w_down):
    x = np.ascontiguousarray(np.asarray(L0_attn_x, dtype=np.float32))
    w_in = np.asarray(L0_attn_w_in, dtype=np.float32)
    w_out = np.asarray(L0_attn_w_out, dtype=np.float32)
    def i8_lowbyte(a):
        a = np.ascontiguousarray(np.asarray(a, dtype="<i4"))
        return a.view(np.int8).reshape(a.shape + (4,))[..., 0]

    w_g = i8_lowbyte(np.asarray(L0_ffn_w_gate, dtype=np.int32).T)
    w_u = i8_lowbyte(np.asarray(L0_ffn_w_up, dtype=np.int32).T)
    w_d = i8_lowbyte(np.asarray(L0_ffn_w_down, dtype=np.int32).T)
    w_g = np.ascontiguousarray(
        w_g.reshape(8, 128, 24, 128).transpose(2, 1, 0, 3).reshape(24, 128, 1024))
    w_u = np.ascontiguousarray(
        w_u.reshape(8, 128, 24, 128).transpose(2, 1, 0, 3).reshape(24, 128, 1024))
    w_d = np.ascontiguousarray(
        w_d.reshape(12, 2, 128, 1024).transpose(0, 2, 1, 3).reshape(12, 128, 2048))

    qaug, masks, ident = host_constants()

    wq = w_in[0:E]
    wk = w_in[E:2 * E]
    wv = w_in[2 * E:3 * E]
    wkvp = np.zeros((NPASS, E, 512), np.float32)
    wqp = np.zeros((NPASS, E, 256), np.float32)
    for pp in range(NPASS):
        hs = slice(pp * 256, pp * 256 + 256)
        wkvp[pp, :, 0:256] = wk[hs].T
        wkvp[pp, :, 256:512] = wv[hs].T
        wqp[pp] = wq[hs].T
    wop = np.ascontiguousarray(w_out.T)

    bf16 = ml_dtypes.bfloat16
    ka_base = np.zeros((NPASS, 8, 3, HPP * 1024), dtype=bf16)
    kw = np.arange(WIN, dtype=np.float64).reshape(8, 1024)
    for pp in range(NPASS):
        for hl in range(HPP):
            sl = SLOPES[pp * HPP + hl]
            for c in range(8):
                a1, a2, a3 = split3(sl * kw[c])
                ka_base[pp, c, 0, hl * 1024:(hl + 1) * 1024] = a1
                ka_base[pp, c, 1, hl * 1024:(hl + 1) * 1024] = a2
                ka_base[pp, c, 2, hl * 1024:(hl + 1) * 1024] = a3

    nc = build_nc()
    in_maps = []
    for core in range(NCORES):
        b, qq = core // 4, core % 4
        q_end = (qq + 1) * QOWN
        pad = WIN - q_end
        xT_win = np.zeros((E, WIN), np.float32)
        xT_win[:, pad:] = x[b, 0:q_end].T
        xo = np.ascontiguousarray(x[b, qq * QOWN:q_end])
        ka = ka_base.copy()
        if pad > 0:
            kmask = np.arange(WIN).reshape(8, 1024) < pad
            for c in range(8):
                rows = kmask[c]
                if rows.any():
                    for hl in range(HPP):
                        blk = slice(hl * 1024, hl * 1024 + 1024)
                        a = ka[:, c, 0, blk]
                        a[:, rows] = NEG
                        ka[:, c, 0, blk] = a
                        ka[:, c, 1, blk][:, rows] = 0.0
                        ka[:, c, 2, blk][:, rows] = 0.0
        in_maps.append({
            "xT": xT_win, "xo": xo, "kac": ka, "qaug": qaug, "masks": masks,
            "identc": ident, "wkvp": wkvp, "wqp": wqp, "wop": wop,
            "wgp": w_g, "wup": w_u, "wdp": w_d,
        })

    res = run_bass_kernel_spmd(nc, in_maps, core_ids=list(range(NCORES)))
    out = np.empty((B, L, E), np.float32)
    for core in range(NCORES):
        b, qq = core // 4, core % 4
        out[b, qq * QOWN:(qq + 1) * QOWN] = res.results[core]["y"]
    return out


if __name__ == "__main__":
    build_nc()
    print("build ok")
